# version 21
# baseline (speedup 1.0000x reference)
import numpy as np
import ml_dtypes
from contextlib import ExitStack

import concourse.bass as bass
import concourse.mybir as mybir
from concourse.bass_utils import run_bass_kernel_spmd

F32 = mybir.dt.float32
F32R = mybir.dt.float32r
BF16 = mybir.dt.bfloat16
E4 = mybir.dt.float8e4
AF = mybir.ActivationFunctionType
ALU = mybir.AluOpType
DRMODE = mybir.MatmulPerfMode.DoubleRow

BFNP = ml_dtypes.bfloat16
E4NP = ml_dtypes.float8_e4m3fn

C1 = float(np.float32(BFNP(0.044715)))
C2 = float(np.float32(BFNP(np.sqrt(2.0 / np.pi))))
EPS = 1e-5
EXP_SHIFT = -2.0
RSQ_DH = float(1.0 / np.sqrt(128.0))


class CFG:
    def __init__(self, G=4, NG=2, D=2048, NT=1024, TTW=512, R=512, H=16, HID=8192):
        self.G = G
        self.NG = NG
        self.NC = G * NG
        self.D = D
        self.DC = D // 128
        self.NT = NT
        self.TTW = TTW
        self.NTT = NT // TTW
        self.R = R
        self.RC = R // 128
        self.H = H
        self.DH = 128
        self.HID = HID
        self.HC = HID // 128
        self.BND = TTW // 128
        self.S = G * NT
        self.NTB = NT // 128
        self.SBLK = self.S // 128
        self.MW = 128 * (self.BND - 1) + TTW


FULL = CFG()

ENGINES = ("tensor", "vector", "scalar", "sync", "gpsimd")


class B:

    def __init__(self, nc, es):
        self.nc = nc
        self.es = es
        self.sems = {}
        self.cnt = {}
        self.state = {}
        self.role = None
        self.eng = None

    def mksem(self, *names):
        for n in names:
            self.sems[n] = self.es.enter_context(self.nc.semaphore(n))
            self.cnt[n] = 0

    def v(self, name):
        return self.cnt[name]

    def op(self, engine, emit, waits=(), inc=None, drain=True):
        if self.role == engine:
            if drain and engine in ("vector", "scalar"):
                self.eng.drain()
            for (s, val) in waits:
                if val > 0:
                    self.eng.wait_ge(self.sems[s], val)
            ins = emit(self.eng)
            if inc is not None and ins is not None:
                ins.then_inc(self.sems[inc[0]], inc[1])
        if inc is not None:
            self.cnt[inc[0]] += inc[1]

    def phase(self, name, fn):
        fns = fn if isinstance(fn, (list, tuple)) else [fn]
        import copy
        snap = dict(self.cnt)
        snap_state = copy.deepcopy(self.state)
        ends = []
        with self.nc.Block(name) as blk:
            for role in ENGINES:
                self.cnt = dict(snap)
                self.state = copy.deepcopy(snap_state)
                self.role = role

                def runner(eng, fns=fns):
                    self.eng = eng
                    for f in fns:
                        f()

                getattr(blk, role)(runner)
                ends.append((self.cnt, self.state))
        for e in ends[1:]:
            assert e == ends[0], f"phase {name}: inconsistent sem counters/state"
        self.cnt, self.state = ends[0]
        self.role = None


def build_program(cfg: CFG):
    nc = bass.Bass("TRN2", target_bir_lowering=False)
    c = cfg
    NT, TTW, NTT, DC, RC, H, HC, G = c.NT, c.TTW, c.NTT, c.DC, c.RC, c.H, c.HC, c.G

    d_x = nc.dram_tensor("x", [128, DC, NT], F32, kind="ExternalInput")
    d_wg = nc.dram_tensor("wg", [HC, 128, 2 * DC * 128], F32, kind="ExternalInput")
    d_wlin = nc.dram_tensor("wlin", [DC, 128, HC * 128], F32, kind="ExternalInput")
    d_wqkv = nc.dram_tensor("wqkv", [128, 3 * DC * RC * 128], F32, kind="ExternalInput")
    d_wqu = nc.dram_tensor("wqu", [H, 128, c.R], F32, kind="ExternalInput")
    d_wku = nc.dram_tensor("wku", [H, 128, c.R], F32, kind="ExternalInput")
    d_wvu = nc.dram_tensor("wvu", [H, 128, RC * 128], F32, kind="ExternalInput")
    d_wout = nc.dram_tensor("wout", [DC, 128, H * 128], F32, kind="ExternalInput")
    d_outb = nc.dram_tensor("outb", [128, DC], F32, kind="ExternalInput")
    d_mask = nc.dram_tensor("maskbig", [128, G, c.MW], BF16, kind="ExternalInput")
    d_cvec = nc.dram_tensor("cvec", [128, 8], F32, kind="ExternalInput")
    d_y = nc.dram_tensor("y", [128, DC, NT], F32, kind="ExternalOutput")
    d_wg_e4 = nc.dram_tensor("wg_e4", [HC, 128, 2 * DC * 128], E4)
    d_wlin_e4 = nc.dram_tensor("wlin_e4", [DC, 128, HC * 128], E4)
    d_xspill = nc.dram_tensor("xspill", [128, DC, NT], F32)
    d_ccin = nc.dram_tensor("cc_in", [2, RC, 128, NT], BF16)
    d_ccout = nc.dram_tensor("cc_out", [G, 2, RC, 128, NT], BF16)
    d_ccin_k = d_ccin.ap()[0]
    d_ccin_v = d_ccin.ap()[1]

    es_root = ExitStack()
    b = B(nc, es_root)
    b.mksem("sLX", "sCIN", "sSP", "sY", "sCW0", "sCW1", "sCC", "sKV",
            "sWG0", "sWG1", "sWL0", "sWL1", "sWU0", "sWU1", "sWO0", "sWO1",
            "sWG0b", "sWG1b", "sWL0b", "sWL1b", "sWA0", "sWA1", "sCIK", "sKV2", "sPE", "sVE", "sSE")
    S = b.sems

    with es_root:
        sb = lambda n, sh, dt: es_root.enter_context(nc.sbuf_tensor(n, sh, dt))
        t_cvec = sb("cvec_sb", [128, 8], F32)
        t_outb = sb("outb_sb", [128, DC], F32)
        t_ones_f = sb("ones_f", [128, 1], F32)
        t_ones_b = sb("ones_b", [128, 1], BF16)
        t_ones_r = sb("ones_r", [1, 128], F32)
        t_stats = sb("stats_sb", [1, 6 * TTW], F32)
        t_scrF = sb("scrF", [128, TTW], F32)
        t_sq = sb("sqscr", [128, 2 * TTW], F32)
        t_tA = sb("tA", [128, NT], BF16)
        t_tB = sb("tB", [128, NT], BF16)
        t_gbf = sb("g_bf", [128, NT], BF16)
        t_f1bf = sb("f1_bf", [128, NT], BF16)
        t_ffnbf = sb("ffn_bf", [128, 2 * TTW], BF16)

        PB = [es_root.enter_context(nc.psum_tensor(f"pb{i}", [128, 512], F32))
              for i in range(8)]
        b.state["bank"] = {i: None for i in range(8)}
        b.state["cw"] = {0: 0, 1: 0}

        class _BG:
            def __getitem__(self, i):
                return b.state["bank"][i]

            def __setitem__(self, i, v):
                b.state["bank"][i] = list(v) if v is not None else None

        bank_guard = _BG()

        class _CW:
            def __getitem__(self, i):
                return b.state["cw"][i]

            def __setitem__(self, i, v):
                b.state["cw"][i] = v

        cw_count_proxy = _CW()

        def guard_waits(banks):
            w = []
            for i in banks:
                g = b.state["bank"][i]
                if g is not None:
                    w.append(tuple(g))
            return w

        def p_preload(t_X):
            b.op("sync", lambda e: e.dma_start(out=t_cvec[:, :], in_=d_cvec.ap()), inc=("sLX", 16))
            b.op("sync", lambda e: e.dma_start(out=t_outb[:, :], in_=d_outb.ap()), inc=("sLX", 16))
            hDC = DC // 2
            b.op("sync", lambda e: e.dma_start(
                out=t_X[:, 0:hDC * NT].rearrange("p (c n) -> p c n", c=hDC),
                in_=d_x.ap()[:, 0:hDC]),
                inc=("sLX", 16))
            b.op("scalar", lambda e: e.dma_start(
                out=t_X[:, hDC * NT:DC * NT].rearrange("p (c n) -> p c n", c=hDC),
                in_=d_x.ap()[:, hDC:DC]),
                inc=("sLX", 16), drain=False)
            b.op("vector", lambda e: e.memset(t_ones_f[:, :], 1.0), inc=("sVE", 1))
            b.op("vector", lambda e: e.memset(t_ones_b[:, :], 1.0), inc=("sVE", 1))
            b.op("vector", lambda e: e.memset(t_ones_r[:, :], 1.0), inc=("sVE", 1))

        def p_ln(t_X, x_ready_waits, t_out):
            apply_done = []
            for T in range(NTT):
                bk = 4 * (T % 2)
                pS1, pS2, pMB, pIB = PB[bk], PB[bk + 1], PB[bk + 2], PB[bk + 3]

                def st(k):
                    return t_stats[:, k * TTW:(k + 1) * TTW]

                s2done = []
                for dc in range(DC):
                    xs = t_X[:, dc * NT + T * TTW: dc * NT + T * TTW + TTW]
                    sqs = t_sq[:, (dc % 2) * TTW:(dc % 2) * TTW + TTW]
                    wsq = list(x_ready_waits) if dc < 2 else [("sPE", s2done[dc - 2])]
                    b.op("scalar", lambda e, xs=xs, sqs=sqs: e.square(sqs, xs),
                         waits=wsq, inc=("sSE", 1), drain=False)
                    b.op("tensor", lambda e, xs=xs, pS1=pS1, dc=dc: e.matmul(
                        pS1[0:1, 0:TTW], t_ones_f[:, :], xs,
                        start=(dc == 0), stop=(dc == DC - 1)),
                        waits=(list(x_ready_waits) + guard_waits([bk])) if dc == 0 else (),
                        inc=("sPE", 1) if dc == DC - 1 else None)
                    b.op("tensor", lambda e, sqs=sqs, pS2=pS2, dc=dc: e.matmul(
                        pS2[0:1, 0:TTW], t_ones_f[:, :], sqs,
                        start=(dc == 0), stop=(dc == DC - 1)),
                        waits=[("sSE", b.v("sSE"))] + (guard_waits([bk + 1]) if dc == 0 else []),
                        inc=("sPE", 1))
                    s2done.append(b.v("sPE"))
                b.op("vector", lambda e, pS1=pS1: e.tensor_scalar(
                    st(0), pS1[0:1, 0:TTW], 1.0 / c.D, None, ALU.mult),
                    waits=[("sPE", b.v("sPE"))], inc=("sVE", 1))
                b.op("vector", lambda e, pS2=pS2: e.tensor_scalar(
                    st(1), pS2[0:1, 0:TTW], 1.0 / c.D, None, ALU.mult), inc=("sVE", 1))
                bank_guard[bk] = ("sVE", b.v("sVE"))
                bank_guard[bk + 1] = ("sVE", b.v("sVE"))
                b.op("vector", lambda e: e.tensor_tensor(st(2), st(0), st(0), ALU.mult), inc=("sVE", 1))
                b.op("vector", lambda e: e.tensor_tensor(st(2), st(1), st(2), ALU.subtract), inc=("sVE", 1))
                b.op("vector", lambda e: e.tensor_scalar(st(2), st(2), EPS, None, ALU.add), inc=("sVE", 1))
                b.op("scalar", lambda e: e.activation(st(3), st(2), AF.Sqrt),
                     waits=[("sVE", b.v("sVE"))], inc=("sSE", 1), drain=False)
                b.op("vector", lambda e: e.reciprocal(st(4), st(3)),
                     waits=[("sSE", b.v("sSE"))], inc=("sVE", 1))
                for _ in range(2):
                    b.op("vector", lambda e: e.tensor_tensor(st(5), st(4), st(4), ALU.mult), inc=("sVE", 1))
                    b.op("vector", lambda e: e.tensor_tensor(st(5), st(5), st(2), ALU.mult), inc=("sVE", 1))
                    b.op("vector", lambda e: e.tensor_scalar(st(5), st(5), -0.5, 1.5, ALU.mult, ALU.add), inc=("sVE", 1))
                    b.op("vector", lambda e: e.tensor_tensor(st(4), st(4), st(5), ALU.mult), inc=("sVE", 1))
                b.op("tensor", lambda e, pMB=pMB: e.matmul(
                    pMB[:, 0:TTW], t_ones_r[0:1, :], st(0), start=True, stop=True),
                    waits=[("sVE", b.v("sVE"))] + guard_waits([bk + 2]), inc=("sPE", 1))
                b.op("tensor", lambda e, pIB=pIB: e.matmul(
                    pIB[:, 0:TTW], t_ones_r[0:1, :], st(4), start=True, stop=True),
                    waits=guard_waits([bk + 3]), inc=("sPE", 1))
                for dc in range(DC):
                    xs = t_X[:, dc * NT + T * TTW: dc * NT + T * TTW + TTW]
                    outs = t_out[:, dc * NT + T * TTW: dc * NT + T * TTW + TTW]
                    b.op("vector", lambda e, xs=xs, pMB=pMB: e.tensor_tensor(
                        t_scrF[:, :], xs, pMB[:, 0:TTW], ALU.subtract),
                        waits=[("sPE", b.v("sPE"))] if dc == 0 else (), inc=("sVE", 1))
                    b.op("vector", lambda e, outs=outs, pIB=pIB: e.tensor_tensor(
                        outs, t_scrF[:, :], pIB[:, 0:TTW], ALU.mult), inc=("sVE", 1))
                bank_guard[bk + 2] = ("sVE", b.v("sVE"))
                bank_guard[bk + 3] = ("sVE", b.v("sVE"))
                apply_done.append(b.v("sVE"))
            return apply_done

        cw_count = cw_count_proxy

        def p_ffn(t_X, t_xhat, xhat_ready, t_act, wgbuf, wlbuf, cache_write,
                  space_ready=()):
            sfx = "" if cache_write else "b"
            fill_val = {}
            pe_done_h = {}
            ve_done_h = {}
            prev_cast_se = {"v": 0}

            def fill_wg(hf, wts):
                bufp = hf % 2
                if cache_write:
                    if cw_count[bufp] > 0:
                        wts = wts + [(f"sCW{bufp}", 16 * cw_count[bufp])]
                    b.op("gpsimd", lambda e, hf=hf, bufp=bufp: e.dma_start(
                        out=wgbuf[bufp][:, :], in_=d_wg.ap()[hf]),
                        waits=wts, inc=(f"sWG{bufp}{sfx}", 16))
                else:
                    b.op("sync", lambda e, hf=hf, bufp=bufp: e.dma_start(
                        out=wgbuf[bufp][:, :], in_=d_wg_e4.ap()[hf]),
                        waits=wts, inc=(f"sWG{bufp}{sfx}", 16))
                fill_val[hf] = (f"sWG{bufp}{sfx}", b.v(f"sWG{bufp}{sfx}"))
                if cache_write:
                    b.op("sync", lambda e, hf=hf, bufp=bufp: e.dma_start(
                        out=d_wg_e4.ap()[hf], in_=wgbuf[bufp][:, :]),
                        waits=[fill_val[hf]],
                        inc=(f"sCW{bufp}", 16))
                    cw_count[bufp] += 1

            for hf in range(min(2, HC)):
                fill_wg(hf, list(space_ready))
            for h in range(HC):
                bufp = h % 2
                bb = 4 * (h % 2)
                first_w = [fill_val[h]]
                if h >= 2:
                    first_w.append(("sVE", ve_done_h[h - 2]))
                first_w += guard_waits([bb + i for i in range(2 * NTT)])
                first = True
                firstT = [True] * NTT
                for gate in range(2):
                    for j in range(DC // 2):
                        base = (gate * DC + 2 * j) * 128
                        lh = wgbuf[bufp][:, base:base + 256].rearrange("p (i m) -> p i m", i=2)
                        xsl = t_xhat[:, 2 * j * NT:(2 * j + 2) * NT].rearrange(
                            "p (i n) -> p i n", i=2)
                        for T in range(NTT):
                            pb = PB[bb + gate * NTT + T]
                            rhs = xsl[:, :, T * TTW:T * TTW + TTW]
                            stop = (j == DC // 2 - 1)
                            w_ = list(first_w) if first else []
                            if firstT[T]:
                                w_.append(("sVE", xhat_ready[T]))
                                firstT[T] = False
                            b.op("tensor", lambda e, pb=pb, lh=lh, rhs=rhs, j=j, stop=stop: e.matmul(
                                pb[:, 0:TTW], lh, rhs, start=(j == 0), stop=stop,
                                perf_mode=DRMODE),
                                waits=w_,
                                inc=("sPE", 1) if stop else None)
                            first = False
                pe_done_h[h] = b.v("sPE")
                if h + 2 < HC:
                    fill_wg(h + 2, [("sPE", pe_done_h[h])])
                wv = [("sPE", b.v("sPE"))]
                for T in range(NTT):
                    pbg = PB[bb + T]
                    pbf = PB[bb + NTT + T]
                    b.op("vector", lambda e, T=T, pbg=pbg: e.tensor_copy(
                        t_gbf[:, T * TTW:T * TTW + TTW], pbg[:, 0:TTW]),
                        waits=wv if T == 0 else (), inc=("sVE", 1))
                    b.op("scalar", lambda e, T=T, pbf=pbf: e.activation(
                        t_f1bf[:, T * TTW:T * TTW + TTW], pbf[:, 0:TTW], AF.Identity),
                        waits=(wv + [("sVE", b.v("sVE"))]) if T == 0 else (),
                        inc=("sSE", 1), drain=False)
                ve_done_h[h] = b.v("sVE")
                se_copy_h = b.v("sSE")
                for i in range(2 * NTT):
                    bank_guard[bb + i] = ("sVE", ve_done_h[h]) if i < NTT else ("sSE", se_copy_h)
                b.op("scalar", lambda e: e.square(t_tA[:, :], t_gbf[:, :]),
                     waits=[("sVE", b.v("sVE")), ("sSE", prev_cast_se["v"])],
                     inc=("sSE", 1), drain=False)
                b.op("vector", lambda e: e.tensor_tensor(t_tB[:, :], t_tA[:, :], t_gbf[:, :], ALU.mult),
                     waits=[("sSE", b.v("sSE"))], inc=("sVE", 1), drain=False)
                b.op("vector", lambda e: e.tensor_scalar(
                    t_tA[:, :], t_tB[:, :], C1, None, ALU.mult), inc=("sVE", 1))
                b.op("vector", lambda e: e.tensor_tensor(
                    t_tB[:, :], t_gbf[:, :], t_tA[:, :], ALU.add), inc=("sVE", 1))
                b.op("vector", lambda e: e.tensor_scalar(
                    t_tA[:, :], t_tB[:, :], C2, None, ALU.mult), inc=("sVE", 1))
                b.op("scalar", lambda e: e.activation(t_tB[:, :], t_tA[:, :], AF.Tanh),
                     waits=[("sVE", b.v("sVE"))], inc=("sSE", 1), drain=False)
                b.op("vector", lambda e: e.tensor_scalar(
                    t_tA[:, :], t_tB[:, :], 1.0, None, ALU.add),
                    waits=[("sSE", b.v("sSE"))], inc=("sVE", 1), drain=False)
                b.op("vector", lambda e: e.scalar_tensor_tensor(
                    t_tB[:, :], t_tA[:, :], 0.5, t_gbf[:, :], ALU.mult, ALU.mult), inc=("sVE", 1))
                b.op("vector", lambda e: e.tensor_tensor(
                    t_tA[:, :], t_tB[:, :], t_f1bf[:, :], ALU.mult),
                    waits=[("sSE", se_copy_h)], inc=("sVE", 1))
                b.op("scalar", lambda e, h=h: e.copy(
                    t_act[:, h * NT:(h + 1) * NT], t_tA[:, :]),
                    waits=[("sVE", b.v("sVE"))], inc=("sSE", 1), drain=False)
                prev_cast_se["v"] = b.v("sSE")
            act_ready = ("sSE", b.v("sSE"))

            fill3_val = {}
            pe_done_oc = {}
            ve_done_oc = {}

            def fill_wl(of, wts):
                bufp = of % 2
                if cache_write:
                    if cw_count[bufp] > 0:
                        wts = wts + [(f"sCW{bufp}", 16 * cw_count[bufp])]
                    b.op("gpsimd", lambda e, of=of, bufp=bufp: e.dma_start(
                        out=wlbuf[bufp][:, :], in_=d_wlin.ap()[of]),
                        waits=wts, inc=(f"sWL{bufp}{sfx}", 16))
                else:
                    b.op("sync", lambda e, of=of, bufp=bufp: e.dma_start(
                        out=wlbuf[bufp][:, :], in_=d_wlin_e4.ap()[of]),
                        waits=wts, inc=(f"sWL{bufp}{sfx}", 16))
                fill3_val[of] = (f"sWL{bufp}{sfx}", b.v(f"sWL{bufp}{sfx}"))
                if cache_write:
                    b.op("sync", lambda e, of=of, bufp=bufp: e.dma_start(
                        out=d_wlin_e4.ap()[of], in_=wlbuf[bufp][:, :]),
                        waits=[fill3_val[of]],
                        inc=(f"sCW{bufp}", 16))
                    cw_count[bufp] += 1

            for of in range(min(2, DC)):
                gate_done = [("sPE", pe_done_h[HC - 2 + (of % 2)])]
                fill_wl(of, list(space_ready) + gate_done)
            for oc in range(DC):
                bufp = oc % 2
                banks = [2 * (oc % 2) + T for T in range(NTT)]
                first_w = [fill3_val[oc], act_ready]
                if oc >= 2:
                    first_w.append(("sVE", ve_done_oc[oc - 2]))
                first_w += guard_waits(banks)
                first = True
                for jc in range(HC // 2):
                    lh = wlbuf[bufp][:, 2 * jc * 128:2 * jc * 128 + 256].rearrange(
                        "p (i m) -> p i m", i=2)
                    asl = t_act[:, 2 * jc * NT:(2 * jc + 2) * NT].rearrange(
                        "p (i n) -> p i n", i=2)
                    for T in range(NTT):
                        pb = PB[banks[T]]
                        rhs = asl[:, :, T * TTW:T * TTW + TTW]
                        stop = (jc == HC // 2 - 1)
                        b.op("tensor", lambda e, pb=pb, lh=lh, rhs=rhs, jc=jc, stop=stop: e.matmul(
                            pb[:, 0:TTW], lh, rhs, start=(jc == 0), stop=stop,
                            perf_mode=DRMODE),
                            waits=first_w if first else (),
                            inc=("sPE", 1) if stop else None)
                        first = False
                pe_done_oc[oc] = b.v("sPE")
                if oc + 2 < DC:
                    fill_wl(oc + 2, [("sPE", pe_done_oc[oc])])
                wv = [("sPE", b.v("sPE"))]
                for T in range(NTT):
                    pb = PB[banks[T]]
                    fb = t_ffnbf[:, T * TTW:T * TTW + TTW]
                    xs = t_X[:, oc * NT + T * TTW: oc * NT + T * TTW + TTW]
                    b.op("vector", lambda e, pb=pb, fb=fb: e.tensor_copy(fb, pb[:, 0:TTW]),
                         waits=wv if T == 0 else (), inc=("sVE", 1))
                    b.op("vector", lambda e, fb=fb, xs=xs: e.scalar_tensor_tensor(
                        xs, fb, 0.5, xs, ALU.mult, ALU.add), inc=("sVE", 1))
                ve_done_oc[oc] = b.v("sVE")
                for i in banks:
                    bank_guard[i] = ("sVE", ve_done_oc[oc])
            return ve_done_oc

        esX = ExitStack()
        t_X = esX.enter_context(nc.sbuf_tensor("X1", [128, DC * NT], F32, side="right"))

        es1 = ExitStack()
        t_xhat_e4 = es1.enter_context(nc.sbuf_tensor("xhat_e4", [128, DC * NT], E4))
        t_act = es1.enter_context(nc.sbuf_tensor("act_e4", [128, HC * NT], E4))
        wbufs = [es1.enter_context(nc.sbuf_tensor(f"wbuf{i}", [128, max(2 * DC, HC) * 128], E4)) for i in range(2)]
        wgbuf = [t[:, 0:2 * DC * 128] for t in wbufs]
        wlbuf = [t[:, 0:HC * 128] for t in wbufs]

        ln1_ve = [None]
        b.phase("blk1", [
            lambda: p_preload(t_X),
            lambda: ln1_ve.__setitem__(0, p_ln(
                t_X, [("sLX", b.v("sLX")), ("sVE", b.v("sVE"))], t_xhat_e4)),
            lambda: p_ffn(t_X, t_xhat_e4, ln1_ve[0], t_act, wgbuf, wlbuf, True),
        ])
        

        es1.close()

        esQ = ExitStack()
        t_Qd = esQ.enter_context(nc.sbuf_tensor("Qd", [128, RC * NT], BF16))
        t_wqu_b = [esQ.enter_context(nc.sbuf_tensor(f"wqu_b{i}", [128, c.R], BF16)) for i in range(2)]
        t_wku_b = [esQ.enter_context(nc.sbuf_tensor(f"wku_b{i}", [128, c.R], BF16)) for i in range(2)]
        t_wvu2 = esQ.enter_context(nc.sbuf_tensor("wvu2", [128, RC * 2 * 128], BF16))
        es2 = ExitStack()
        t_xhat_bf = es2.enter_context(nc.sbuf_tensor("xhat_bf", [128, DC * NT], BF16))
        t_wqkv = [es2.enter_context(nc.sbuf_tensor(f"wqkv_sb{i}", [128, DC * RC * 128], BF16))
                  for i in range(2)]
        t_Kd = es2.enter_context(nc.sbuf_tensor("Kd_loc", [128, RC * NT], BF16))
        t_Vd = es2.enter_context(nc.sbuf_tensor("Vd_loc", [128, RC * NT], BF16))

        resid1 = b.v("sVE")
        cw_final = [("sCW0", b.v("sCW0")), ("sCW1", b.v("sCW1"))]
        ln2_ve = [None]

        wu_val0 = {}

        def phase_qkv():
            pe_proj_done = {}
            wfill = {}

            def fill_wqkv(p, wts):
                par = p % 2
                view = d_wqkv.ap()[:, p * DC * RC * 128:(p + 1) * DC * RC * 128]
                b.op("gpsimd", lambda e, par=par, view=view: e.dma_start(
                    out=t_wqkv[par][:, :], in_=view), waits=wts, inc=(f"sWA{par}", 16))
                wfill[p] = (f"sWA{par}", b.v(f"sWA{par}"))

            fill_wqkv(0, cw_final)
            fill_wqkv(1, [])
            hDC = DC // 2
            b.op("sync", lambda e: e.dma_start(
                out=d_xspill.ap()[:, 0:hDC],
                in_=t_X[:, 0:hDC * NT].rearrange("p (c n) -> p c n", c=hDC)),
                waits=[("sVE", ln2_ve[0][-1])], inc=("sSP", 16))
            b.op("scalar", lambda e: e.dma_start(
                out=d_xspill.ap()[:, hDC:DC],
                in_=t_X[:, hDC * NT:DC * NT].rearrange("p (c n) -> p c n", c=hDC)),
                waits=[("sVE", ln2_ve[0][-1])], inc=("sSP", 16), drain=False)
            outs = {0: t_Kd, 1: t_Vd, 2: t_Qd}
            kd_vd_done = [0]
            for proj in range(3):
                for rc in range(RC):
                    for T in range(NTT):
                        k = (proj * RC * NTT + rc * NTT + T) % 4
                        pb = PB[k]
                        first_w = [wfill[proj], ("sVE", ln2_ve[0][T])] + guard_waits([k])
                        for dc in range(DC):
                            lh = t_wqkv[proj % 2][:, (dc * RC + rc) * 128:
                                                  (dc * RC + rc) * 128 + 128]
                            rhs = t_xhat_bf[:, dc * NT + T * TTW: dc * NT + T * TTW + TTW]
                            stop = (dc == DC - 1)
                            b.op("tensor", lambda e, pb=pb, lh=lh, rhs=rhs, dc=dc, stop=stop: e.matmul(
                                pb[:, 0:TTW], lh, rhs, start=(dc == 0), stop=stop),
                                waits=first_w if dc == 0 else (),
                                inc=("sPE", 1) if stop else None)
                        dst = outs[proj][:, rc * NT + T * TTW: rc * NT + T * TTW + TTW]
                        b.op("vector", lambda e, dst=dst, pb=pb: e.tensor_copy(dst, pb[:, 0:TTW]),
                             waits=[("sPE", b.v("sPE"))], inc=("sVE", 1), drain=False)
                        bank_guard[k] = ("sVE", b.v("sVE"))
                pe_proj_done[proj] = b.v("sPE")
                if proj + 2 < 3:
                    fill_wqkv(proj + 2, [("sPE", pe_proj_done[proj])])
                if proj == 0:
                    b.op("sync", lambda e: e.dma_start(
                        out=d_ccin_k.transpose([1, 0, 2]),
                        in_=t_Kd[:, :].rearrange("p (rc t) -> p rc t", rc=RC)),
                        waits=[("sVE", b.v("sVE"))], inc=("sCIK", 16))
                if proj == 1:
                    kd_vd_done[0] = b.v("sVE")
            b.op("sync", lambda e: e.dma_start(
                out=d_ccin_v.transpose([1, 0, 2]),
                in_=t_Vd[:, :].rearrange("p (rc t) -> p rc t", rc=RC)),
                waits=[("sVE", kd_vd_done[0])], inc=("sCIN", 16))

        b.phase("blk2", [
            lambda: ln2_ve.__setitem__(
                0, p_ln(t_X, [("sVE", resid1)] + cw_final, t_xhat_bf)),
            phase_qkv,
        ])
        es2.close()
        esX.close()

        esC = ExitStack()
        t_ctx = esC.enter_context(nc.sbuf_tensor("ctx", [128, H * NT], BF16))
        es5 = ExitStack()
        t_mask = es5.enter_context(nc.sbuf_tensor("mask_sb", [128, G * c.MW], BF16))
        t_binv = es5.enter_context(nc.sbuf_tensor("binv", [128, TTW], F32))
        t_Kda = es5.enter_context(nc.sbuf_tensor("Kd_all", [128, RC * c.S], BF16))
        t_Vda = es5.enter_context(nc.sbuf_tensor("Vd_all", [128, RC * c.S], BF16))
        t_Qm = [es5.enter_context(nc.sbuf_tensor(f"Qm{i}", [128, RC * NT], BF16)) for i in range(2)]
        t_M = [es5.enter_context(nc.sbuf_tensor(f"M_sb{i}", [128, RC * c.R], BF16)) for i in range(2)]
        t_Vup = es5.enter_context(nc.sbuf_tensor("Vup", [128, c.SBLK * 2 * 128], BF16))
        t_E = es5.enter_context(nc.sbuf_tensor("E_sb", [128, 4 * TTW], BF16))


        def phase_kvload():
            for hh in range(min(2, H)):
                par = hh % 2
                wts = [("sCIK", 16), ("sCIN", 16), ("sSP", 32)] if hh == 0 else []
                b.op("gpsimd", lambda e, hh=hh, par=par: e.dma_start(
                    out=t_wqu_b[par][:, :], in_=d_wqu.ap()[hh]), waits=wts, inc=(f"sWU{par}", 16))
                b.op("gpsimd", lambda e, hh=hh, par=par: e.dma_start(
                    out=t_wku_b[par][:, :], in_=d_wku.ap()[hh]), inc=(f"sWU{par}", 16))
                view = t_wvu2[:, :].rearrange("p (rc q n) -> p rc q n", q=2, n=128)[:, :, par]
                b.op("gpsimd", lambda e, hh=hh, view=view: e.dma_start(
                    out=view, in_=d_wvu.ap()[hh].rearrange("p (rc n) -> p rc n", n=128)),
                    inc=(f"sWU{par}", 16))
                wu_val0[hh] = (f"sWU{par}", b.v(f"sWU{par}"))
            rgs = [[g * G + i for i in range(G)] for g in range(c.NG)]
            b.op("gpsimd", lambda e: e.collective_compute(
                "AllGather", ALU.bypass, replica_groups=rgs,
                ins=[d_ccin.ap().opt()], outs=[d_ccout.ap().opt()]),
                waits=[("sCIK", 16), ("sCIN", 16)], inc=("sCC", 1))
            b.op("sync", lambda e: e.dma_start(
                out=t_mask[:, :].rearrange("p (g w) -> p g w", g=G), in_=d_mask.ap()),
                waits=[("sSP", 32), ("sCIK", 16)], inc=("sKV", 16))
            for kv, dst, sem, engq in ((0, t_Kda, "sKV", "sync"), (1, t_Vda, "sKV2", "scalar")):
                for gi in range(G):
                    view = dst[:, :].rearrange("p (rc g t) -> p rc g t", rc=RC, g=G)[:, :, gi]
                    b.op(engq, lambda e, view=view, gi=gi, kv=kv: e.dma_start(
                        out=view, in_=d_ccout.ap()[gi, kv].transpose([1, 0, 2])),
                        waits=[("sCC", 1)] if gi == 0 else (),
                        inc=(sem, 16), drain=False)


        def phase_heads():
            ctx_all = [0]
            wu_val = {}
            qm_ready = {}
            vup_ready = [0]
            pending_fin = []
            b.state["lg"] = 0

            def emit_loads(hh, wts):
                par = hh % 2
                b.op("gpsimd", lambda e, hh=hh, par=par: e.dma_start(
                    out=t_wqu_b[par][:, :], in_=d_wqu.ap()[hh]), waits=wts, inc=(f"sWU{par}", 16))
                b.op("gpsimd", lambda e, hh=hh, par=par: e.dma_start(
                    out=t_wku_b[par][:, :], in_=d_wku.ap()[hh]), inc=(f"sWU{par}", 16))
                view = t_wvu2[:, :].rearrange("p (rc q n) -> p rc q n", q=2, n=128)[:, :, par]
                b.op("gpsimd", lambda e, hh=hh, view=view: e.dma_start(
                    out=view, in_=d_wvu.ap()[hh].rearrange("p (rc n) -> p rc n", n=128)),
                    inc=(f"sWU{par}", 16))
                wu_val[hh] = (f"sWU{par}", b.v(f"sWU{par}"))

            def emit_mqm(hh):
                par = hh % 2
                for qc in range(RC):
                    pb = PB[qc % 2]
                    b.op("tensor", lambda e, pb=pb, qc=qc, par=par: e.matmul(
                        pb[:, 0:c.R], t_wqu_b[par][:, qc * 128:qc * 128 + 128],
                        t_wku_b[par][:, :], start=True, stop=True),
                        waits=([wu_val[hh]] if qc == 0 else []) + guard_waits([qc % 2]),
                        inc=("sPE", 1))
                    dst = t_M[par][:, qc * c.R:(qc + 1) * c.R]
                    b.op("vector", lambda e, dst=dst, pb=pb: e.tensor_scalar(
                        dst, pb[:, 0:c.R], RSQ_DH, None, ALU.mult),
                        waits=[("sPE", b.v("sPE"))], inc=("sVE", 1), drain=False)
                    bank_guard[qc % 2] = ("sVE", b.v("sVE"))
                m_ready = b.v("sVE")
                for rk in range(RC):
                    for T in range(NTT):
                        k = (rk * NTT + T) % 4
                        pb = PB[k]
                        first_w = [("sVE", m_ready)] + guard_waits([k])
                        for qc in range(RC):
                            lh = t_M[par][:, qc * c.R + rk * 128: qc * c.R + rk * 128 + 128]
                            rhs = t_Qd[:, qc * NT + T * TTW: qc * NT + T * TTW + TTW]
                            stop = (qc == RC - 1)
                            b.op("tensor", lambda e, pb=pb, lh=lh, rhs=rhs, qc=qc, stop=stop: e.matmul(
                                pb[:, 0:TTW], lh, rhs, start=(qc == 0), stop=stop),
                                waits=first_w if qc == 0 else (),
                                inc=("sPE", 1) if stop else None)
                        dst = t_Qm[par][:, rk * NT + T * TTW: rk * NT + T * TTW + TTW]
                        b.op("vector", lambda e, dst=dst, pb=pb: e.tensor_copy(dst, pb[:, 0:TTW]),
                             waits=[("sPE", b.v("sPE"))], inc=("sVE", 1), drain=False)
                        bank_guard[k] = ("sVE", b.v("sVE"))
                qm_ready[par] = b.v("sVE")

            wu_val.update(wu_val0)
            emit_mqm(0)
            if H > 1:
                emit_mqm(1)

            for h in range(H):
                par = h % 2
                if h % 2 == 0:
                    for sbk in range(c.SBLK):
                        pb = PB[2 + (sbk % 2)]
                        first_w = ([("sKV2", 16 * G)] if (h == 0 and sbk == 0) else []) + \
                            ([wu_val[h], wu_val.get(h + 1, wu_val[h])] if sbk == 0 else []) + \
                            guard_waits([2 + (sbk % 2)])
                        for rc in range(RC):
                            lh = t_Vda[:, rc * c.S + sbk * 128: rc * c.S + sbk * 128 + 128]
                            rhs = t_wvu2[:, rc * 256:rc * 256 + 256]
                            stop = (rc == RC - 1)
                            b.op("tensor", lambda e, pb=pb, lh=lh, rhs=rhs, rc=rc, stop=stop: e.matmul(
                                pb[:, 0:256], lh, rhs, start=(rc == 0), stop=stop),
                                waits=first_w if rc == 0 else (),
                                inc=("sPE", 1) if stop else None)
                        dst = t_Vup[:, sbk * 256:(sbk + 1) * 256]
                        b.op("vector", lambda e, dst=dst, pb=pb: e.tensor_copy(dst, pb[:, 0:256]),
                             waits=[("sPE", b.v("sPE"))], inc=("sVE", 1), drain=False)
                        bank_guard[2 + (sbk % 2)] = ("sVE", b.v("sVE"))
                    vup_ready[0] = b.v("sVE")
                    if h + 2 < H:
                        emit_loads(h + 2, [("sPE", b.v("sPE"))])
                    if h + 3 < H:
                        emit_loads(h + 3, [("sPE", b.v("sPE"))])
                for T in range(NTT):
                    pbC = PB[4 + (T % 2)]
                    pbL = PB[6 + (T % 2)]
                    blocks = [(r, m) for m in range(c.BND * (T + 1)) for r in range(G)]
                    n_bl = len(blocks)
                    exp_done = {}
                    edep = {}
                    ctxsum_done = {}

                    lgslot = {}

                    def emit_logits(i):
                        r, m = blocks[i]
                        lgslot[i] = b.state["lg"] % 4
                        b.state["lg"] += 1
                        pbLg = PB[lgslot[i]]
                        first_w = (([("sVE", qm_ready[par]), ("sVE", vup_ready[0])] +
                                    ([("sKV", 16 * (G + 1))] if (h == 0 and T == 0) else []))
                                   if i == 0 else []) + guard_waits([lgslot[i]])
                        for rc in range(RC):
                            lh = t_Kda[:, rc * c.S + (r * NT + m * 128):
                                       rc * c.S + (r * NT + m * 128) + 128]
                            rhs = t_Qm[par][:, rc * NT + T * TTW: rc * NT + T * TTW + TTW]
                            stop = (rc == RC - 1)
                            b.op("tensor", lambda e, pbLg=pbLg, lh=lh, rhs=rhs, rc=rc, stop=stop: e.matmul(
                                pbLg[:, 0:TTW], lh, rhs, start=(rc == 0), stop=stop),
                                waits=first_w if rc == 0 else (),
                                inc=("sPE", 1) if stop else None)
                        ew = [("sPE", b.v("sPE"))]
                        if i >= 4:
                            ew.append(("sPE", ctxsum_done[i - 4]))
                        esl = t_E[:, (i % 4) * TTW:(i % 4) * TTW + TTW]
                        b.op("scalar", lambda e, esl=esl, pbLg=pbLg: e.activation(
                            esl, pbLg[:, 0:TTW], AF.Exp, bias=t_cvec[:, 1:2]),
                            waits=ew, inc=("sSE", 1), drain=False)
                        exp_done[i] = b.v("sSE")
                        bank_guard[lgslot[i]] = ("sSE", b.v("sSE"))
                        masked = (m >= c.BND * T)
                        dep = [("sSE", exp_done[i])]
                        if masked:
                            dlt = m - c.BND * T
                            msl = t_mask[:, r * c.MW + 128 * (c.BND - 1 - dlt):
                                         r * c.MW + 128 * (c.BND - 1 - dlt) + TTW]
                            b.op("vector", lambda e, esl=esl, msl=msl: e.tensor_tensor(
                                esl, esl, msl, ALU.mult),
                                waits=[("sSE", b.v("sSE"))], inc=("sVE", 1), drain=False)
                            dep.append(("sVE", b.v("sVE")))
                        edep[i] = (esl, dep)

                    def emit_ctxsum(i):
                        r, m = blocks[i]
                        esl, dep = edep[i]
                        sbid = r * c.NTB + m
                        cw = dep[-1:]
                        if i == 0:
                            cw = list(dep) + guard_waits([4 + (T % 2), 6 + (T % 2)])
                        lhv = t_Vup[:, sbid * 256 + par * 128: sbid * 256 + par * 128 + 128]
                        b.op("tensor", lambda e, pbC=pbC, lhv=lhv, esl=esl, i=i: e.matmul(
                            pbC[:, 0:TTW], lhv, esl, start=(i == 0), stop=(i == n_bl - 1)),
                            waits=cw)
                        b.op("tensor", lambda e, esl=esl, i=i: e.matmul(
                            pbL[0:1, 0:TTW], t_ones_b[:, :], esl,
                            start=(i == 0), stop=(i == n_bl - 1)),
                            inc=("sPE", 1))
                        ctxsum_done[i] = b.v("sPE")

                    for i in range(min(3, n_bl)):
                        emit_logits(i)
                    if pending_fin:
                        pending_fin.pop(0)()
                    for i in range(n_bl):
                        if i + 3 < n_bl:
                            emit_logits(i + 3)
                        emit_ctxsum(i)

                    def finalize(h=h, T=T, pbC=pbC, pbL=pbL,
                                 pe_after=b.v("sPE")):
                        def stl(k):
                            return t_stats[:, k * TTW:(k + 1) * TTW]
                        b.op("vector", lambda e: e.reciprocal(stl(0), pbL[0:1, 0:TTW]),
                             waits=[("sPE", pe_after)], inc=("sVE", 1), drain=False)
                        bank_guard[6 + (T % 2)] = ("sVE", b.v("sVE"))
                        slot = b.state["lg"] % 4
                        b.state["lg"] += 1
                        b.op("tensor", lambda e, slot=slot: e.matmul(
                            PB[slot][:, 0:TTW], t_ones_r[0:1, :], stl(0), start=True, stop=True),
                            waits=[("sVE", b.v("sVE"))] + guard_waits([slot]), inc=("sPE", 1))
                        b.op("vector", lambda e, slot=slot: e.tensor_copy(t_binv[:, :], PB[slot][:, 0:TTW]),
                             waits=[("sPE", b.v("sPE"))], inc=("sVE", 1))
                        bank_guard[slot] = ("sVE", b.v("sVE"))
                        dst = t_ctx[:, h * NT + T * TTW: h * NT + T * TTW + TTW]
                        b.op("vector", lambda e, dst=dst: e.tensor_tensor(
                            dst, pbC[:, 0:TTW], t_binv[:, :], ALU.mult), inc=("sVE", 1))
                        bank_guard[4 + (T % 2)] = ("sVE", b.v("sVE"))

                    pending_fin.append(finalize)
                if h + 2 < H:
                    emit_mqm(h + 2)
            while pending_fin:
                pending_fin.pop(0)()
            ctx_all[0] = b.v("sVE")
            return ctx_all[0]

        ctx_ready = [None]
        b.phase("blk3", [phase_kvload,
                         lambda: ctx_ready.__setitem__(0, phase_heads())])
        es5.close()

        es6 = ExitStack()
        t_X2 = es6.enter_context(nc.sbuf_tensor("X2", [128, DC * NT], F32, side="right"))
        esA = ExitStack()
        t_attnscr = esA.enter_context(nc.sbuf_tensor("attnscr", [128, 2 * TTW], F32))
        t_wo_b = [esA.enter_context(nc.sbuf_tensor(f"wo_b{i}", [128, H * 128], BF16)) for i in range(2)]

        def phase_outproj():
            hDC = DC // 2
            b.op("sync", lambda e: e.dma_start(
                out=t_X2[:, 0:hDC * NT].rearrange("p (c n) -> p c n", c=hDC),
                in_=d_xspill.ap()[:, 0:hDC]),
                waits=[("sSP", 32)], inc=("sSP", 16))
            b.op("scalar", lambda e: e.dma_start(
                out=t_X2[:, hDC * NT:DC * NT].rearrange("p (c n) -> p c n", c=hDC),
                in_=d_xspill.ap()[:, hDC:DC]),
                waits=[("sSP", 32)], inc=("sSP", 16), drain=False)
            fillo_val = {}
            pe_done = {}
            ve_done = {}
            se_done = {}

            def fill_wo(of, wts):
                par = of % 2
                b.op("gpsimd", lambda e, of=of, par=par: e.dma_start(
                    out=t_wo_b[par][:, :], in_=d_wout.ap()[of]),
                    waits=wts, inc=(f"sWO{par}", 16))
                fillo_val[of] = (f"sWO{par}", b.v(f"sWO{par}"))

            for of in range(min(2, DC)):
                fill_wo(of, [])
            for oc in range(DC):
                par = oc % 2
                banks = [2 * (oc % 2) + T for T in range(NTT)]
                first_w = [fillo_val[oc], ("sVE", ctx_ready[0])] + guard_waits(banks)
                first = True
                for hh in range(H):
                    lh = t_wo_b[par][:, hh * 128:hh * 128 + 128]
                    for T in range(NTT):
                        pb = PB[banks[T]]
                        rhs = t_ctx[:, hh * NT + T * TTW: hh * NT + T * TTW + TTW]
                        stop = (hh == H - 1)
                        b.op("tensor", lambda e, pb=pb, lh=lh, rhs=rhs, hh=hh, stop=stop: e.matmul(
                            pb[:, 0:TTW], lh, rhs, start=(hh == 0), stop=stop),
                            waits=first_w if first else (),
                            inc=("sPE", 1) if stop else None)
                        first = False
                pe_done[oc] = b.v("sPE")
                if oc + 2 < DC:
                    fill_wo(oc + 2, [("sPE", pe_done[oc])])
                for T in range(NTT):
                    pb = PB[banks[T]]
                    asl = t_attnscr[:, T * TTW:T * TTW + TTW]
                    xs = t_X2[:, oc * NT + T * TTW: oc * NT + T * TTW + TTW]
                    sw = [("sPE", pe_done[oc])]
                    if oc >= 1:
                        sw.append(("sVE", ve_done[oc - 1]))
                    b.op("scalar", lambda e, asl=asl, pb=pb, oc=oc: e.activation(
                        asl, pb[:, 0:TTW], AF.Identity, bias=t_outb[:, oc:oc + 1]),
                        waits=sw, inc=("sSE", 1), drain=False)
                    vw = [("sSE", b.v("sSE"))]
                    if oc == 0 and T == 0:
                        vw.append(("sSP", 64))
                    b.op("vector", lambda e, asl=asl, xs=xs: e.tensor_tensor(
                        xs, xs, asl, ALU.add), waits=vw, inc=("sVE", 1))
                ve_done[oc] = b.v("sVE")
                se_done[oc] = b.v("sSE")
                for i in banks:
                    bank_guard[i] = ("sSE", se_done[oc])

        b.phase("outproj", phase_outproj)
        esA.close()
        esC.close()

        es7 = ExitStack()
        t_xhat_e4b = es7.enter_context(nc.sbuf_tensor("xhat_e4b", [128, DC * NT], E4))
        t_act2 = es7.enter_context(nc.sbuf_tensor("act_e4b", [128, HC * NT], E4))
        wbufs2 = [es7.enter_context(nc.sbuf_tensor(f"wbuf2_{i}", [128, max(2 * DC, HC) * 128], E4)) for i in range(2)]
        wgbuf2 = [t[:, 0:2 * DC * 128] for t in wbufs2]
        wlbuf2 = [t[:, 0:HC * 128] for t in wbufs2]

        ffn2_oc = [None]

        def phase_store():
            for oc in range(DC):
                b.op("sync", lambda e, oc=oc: e.dma_start(
                    out=d_y.ap()[:, oc],
                    in_=t_X2[:, oc * NT:(oc + 1) * NT]),
                    waits=[("sVE", ffn2_oc[0][oc])] if oc in (0, DC - 1) or True else (),
                    inc=("sY", 16))
            b.op("sync", lambda e: e.nop(), waits=[("sY", 16 * DC)])

        resid2 = b.v("sVE")
        space2 = [("sVE", b.v("sVE")), ("sPE", b.v("sPE")), ("sSE", b.v("sSE"))]
        ln3_ve = [None]
        b.phase("blk5", [
            lambda: ln3_ve.__setitem__(0, p_ln(t_X2, [("sVE", resid2)], t_xhat_e4b)),
            lambda: ffn2_oc.__setitem__(0, p_ffn(
                t_X2, t_xhat_e4b, ln3_ve[0], t_act2, wgbuf2, wlbuf2, False,
                space_ready=space2)),
            phase_store,
        ])

        es7.close()
        es6.close()
        esQ.close()

    return nc



def _shared_weight_layout(cfg, wg, wlin, qd, kd, vd, qu, ku, vu, ow, ob):
    c = cfg
    out = {}
    a = np.ascontiguousarray(
        wg.reshape(2, c.DC // 2, 2, 128, c.HC, 128).transpose(4, 3, 0, 1, 2, 5)
    ).reshape(c.HC, 128, 2 * c.DC * 128)
    out["wg"] = np.ascontiguousarray(a)
    a = np.ascontiguousarray(
        wlin.reshape(c.HC // 2, 2, 128, c.DC, 128).transpose(3, 2, 0, 1, 4)
    ).reshape(c.DC, 128, c.HC * 128)
    out["wlin"] = np.ascontiguousarray(a)
    qkv = []
    for w in (kd, vd, qd):
        t = w.reshape(c.RC, 128, c.DC, 128).transpose(3, 2, 0, 1)
        qkv.append(t)
    out["wqkv"] = np.ascontiguousarray(np.stack(qkv, axis=1)).reshape(128, 3 * c.DC * c.RC * 128)
    out["wqkv"] = np.ascontiguousarray(
        np.stack(qkv, axis=1).transpose(0, 1, 2, 3, 4)).reshape(128, 3 * c.DC * c.RC * 128)
    out["wqu"] = np.ascontiguousarray(qu.reshape(c.H, 128, c.R))
    out["wku"] = np.ascontiguousarray(ku.reshape(c.H, 128, c.R))
    out["wvu"] = np.ascontiguousarray(
        vu.reshape(c.H, 128, c.RC, 128).transpose(0, 3, 2, 1)).reshape(c.H, 128, c.RC * 128)
    out["wout"] = np.ascontiguousarray(
        ow.reshape(c.DC, 128, c.H, 128).transpose(0, 3, 2, 1)).reshape(c.DC, 128, c.H * 128)
    out["outb"] = np.ascontiguousarray(ob.reshape(c.DC, 128).T)
    cvec = np.zeros((128, 8), np.float32)
    cvec[:, 0] = EPS
    cvec[:, 1] = EXP_SHIFT
    cvec[:, 2] = RSQ_DH
    out["cvec"] = cvec
    return out


def _mask_for_rank(cfg, r_c):
    c = cfg
    p = np.arange(128)[:, None]
    g = np.arange(c.MW)[None, :]
    mb = np.zeros((128, c.G, c.MW), BFNP)
    for r in range(c.G):
        dlt = 1 if r > r_c else 0
        mb[:, r, :] = (g >= p + 128 * (c.BND - 1) + dlt).astype(BFNP)
    return mb


def _core_x(cfg, xb, r):
    c = cfg
    xloc = xb[r::c.G][:c.NT]
    return np.ascontiguousarray(
        xloc.T.reshape(c.DC, 128, c.NT).transpose(1, 0, 2))


def run(cfg, inputs, core_ids=None, runner=run_bass_kernel_spmd):
    c = cfg
    x = np.asarray(inputs["L0_attn_x"], np.float32)
    shared = _shared_weight_layout(
        c,
        np.asarray(inputs["L0_ffn_w_gating"], np.float32),
        np.asarray(inputs["L0_ffn_w_linear"], np.float32),
        np.asarray(inputs["L0_attn_q_down_w"], np.float32),
        np.asarray(inputs["L0_attn_k_down_w"], np.float32),
        np.asarray(inputs["L0_attn_v_down_w"], np.float32),
        np.asarray(inputs["L0_attn_q_up_w"], np.float32),
        np.asarray(inputs["L0_attn_k_up_w"], np.float32),
        np.asarray(inputs["L0_attn_v_up_w"], np.float32),
        np.asarray(inputs["L0_attn_out_w"], np.float32),
        np.asarray(inputs["L0_attn_out_b"], np.float32),
    )
    masks = [_mask_for_rank(c, r) for r in range(c.G)]
    in_maps = []
    for core in range(c.NC):
        g, r = core // c.G, core % c.G
        m = dict(shared)
        m["x"] = _core_x(c, x[g], r)
        m["maskbig"] = masks[r]
        in_maps.append(m)
    nc = build_program(c)
    res = runner(nc, in_maps, core_ids=core_ids or list(range(c.NC)))
    out = np.empty((c.NG, c.G * c.NT, c.D), np.float32)
    for core in range(c.NC):
        g, r = core // c.G, core % c.G
        Y = np.asarray(res.results[core]["y"]).reshape(128, c.DC, c.NT)
        out[g, r::c.G] = Y.transpose(2, 1, 0).reshape(c.NT, c.D)
    return out


def kernel(**inputs):
    return run(FULL, inputs)


# version 22
# speedup vs baseline: 1.1075x; 1.1075x over previous
import numpy as np
import ml_dtypes
from contextlib import ExitStack

import concourse.bass as bass
import concourse.mybir as mybir
from concourse.bass_utils import run_bass_kernel_spmd

F32 = mybir.dt.float32
F32R = mybir.dt.float32r
BF16 = mybir.dt.bfloat16
E4 = mybir.dt.float8e4
AF = mybir.ActivationFunctionType
ALU = mybir.AluOpType
DRMODE = mybir.MatmulPerfMode.DoubleRow

BFNP = ml_dtypes.bfloat16
E4NP = ml_dtypes.float8_e4m3fn

C1 = float(np.float32(BFNP(0.044715)))
C2 = float(np.float32(BFNP(np.sqrt(2.0 / np.pi))))
EPS = 1e-5
EXP_SHIFT = -2.0
RSQ_DH = float(1.0 / np.sqrt(128.0))


class CFG:
    def __init__(self, G=4, NG=2, D=2048, NT=1024, TTW=512, R=512, H=16, HID=8192):
        self.G = G
        self.NG = NG
        self.NC = G * NG
        self.D = D
        self.DC = D // 128
        self.NT = NT
        self.TTW = TTW
        self.NTT = NT // TTW
        self.R = R
        self.RC = R // 128
        self.H = H
        self.DH = 128
        self.HID = HID
        self.HC = HID // 128
        self.BND = TTW // 128
        self.S = G * NT
        self.NTB = NT // 128
        self.SBLK = self.S // 128
        self.MW = 128 * (self.BND - 1) + TTW


FULL = CFG()

ENGINES = ("tensor", "vector", "scalar", "sync", "gpsimd")


class B:

    def __init__(self, nc, es):
        self.nc = nc
        self.es = es
        self.sems = {}
        self.cnt = {}
        self.state = {}
        self.role = None
        self.eng = None

    def mksem(self, *names):
        for n in names:
            self.sems[n] = self.es.enter_context(self.nc.semaphore(n))
            self.cnt[n] = 0

    def v(self, name):
        return self.cnt[name]

    def op(self, engine, emit, waits=(), inc=None, drain=True):
        if self.role == engine:
            if drain and engine in ("vector", "scalar"):
                self.eng.drain()
            for (s, val) in waits:
                if val > 0:
                    self.eng.wait_ge(self.sems[s], val)
            ins = emit(self.eng)
            if inc is not None and ins is not None:
                ins.then_inc(self.sems[inc[0]], inc[1])
        if inc is not None:
            self.cnt[inc[0]] += inc[1]

    def phase(self, name, fn):
        fns = fn if isinstance(fn, (list, tuple)) else [fn]
        import copy
        snap = dict(self.cnt)
        snap_state = copy.deepcopy(self.state)
        ends = []
        with self.nc.Block(name) as blk:
            for role in ENGINES:
                self.cnt = dict(snap)
                self.state = copy.deepcopy(snap_state)
                self.role = role

                def runner(eng, fns=fns):
                    self.eng = eng
                    for f in fns:
                        f()

                getattr(blk, role)(runner)
                ends.append((self.cnt, self.state))
        for e in ends[1:]:
            assert e == ends[0], f"phase {name}: inconsistent sem counters/state"
        self.cnt, self.state = ends[0]
        self.role = None


def build_program(cfg: CFG):
    nc = bass.Bass("TRN2", target_bir_lowering=False)
    c = cfg
    NT, TTW, NTT, DC, RC, H, HC, G = c.NT, c.TTW, c.NTT, c.DC, c.RC, c.H, c.HC, c.G

    d_x = nc.dram_tensor("x", [128, DC, NT], F32, kind="ExternalInput")
    d_wg = nc.dram_tensor("wg", [HC, 128, 2 * DC * 128], F32, kind="ExternalInput")
    d_wlin = nc.dram_tensor("wlin", [DC, 128, HC * 128], F32, kind="ExternalInput")
    d_wqkv = nc.dram_tensor("wqkv", [128, 3 * DC * RC * 128], F32, kind="ExternalInput")
    d_wqu = nc.dram_tensor("wqu", [H, 128, c.R], F32, kind="ExternalInput")
    d_wku = nc.dram_tensor("wku", [H, 128, c.R], F32, kind="ExternalInput")
    d_wvu = nc.dram_tensor("wvu", [H, 128, RC * 128], F32, kind="ExternalInput")
    d_wout = nc.dram_tensor("wout", [DC, 128, H * 128], F32, kind="ExternalInput")
    d_outb = nc.dram_tensor("outb", [128, DC], F32, kind="ExternalInput")
    d_mask = nc.dram_tensor("maskbig", [128, G, c.MW], BF16, kind="ExternalInput")
    d_cvec = nc.dram_tensor("cvec", [128, 8], F32, kind="ExternalInput")
    d_y = nc.dram_tensor("y", [128, DC, NT], F32, kind="ExternalOutput")
    d_wg_e4 = nc.dram_tensor("wg_e4", [HC, 128, 2 * DC * 128], E4)
    d_wlin_e4 = nc.dram_tensor("wlin_e4", [DC, 128, HC * 128], E4)
    d_xspill = nc.dram_tensor("xspill", [128, DC, NT], F32)
    d_ccin = nc.dram_tensor("cc_in", [2, RC, 128, NT], BF16)
    d_ccout = nc.dram_tensor("cc_out", [G, 2, RC, 128, NT], BF16)
    d_ccin_k = d_ccin.ap()[0]
    d_ccin_v = d_ccin.ap()[1]

    es_root = ExitStack()
    b = B(nc, es_root)
    b.mksem("sLX", "sCIN", "sSP", "sY", "sCW0", "sCW1", "sCC", "sKV",
            "sWG0", "sWG1", "sWL0", "sWL1", "sWU0", "sWU1", "sWO0", "sWO1",
            "sWG0b", "sWG1b", "sWL0b", "sWL1b", "sWA0", "sWA1", "sCIK", "sKV2", "sKV8",
            "sPE", "sVE", "sSE")
    S = b.sems

    with es_root:
        sb = lambda n, sh, dt: es_root.enter_context(nc.sbuf_tensor(n, sh, dt))
        t_cvec = sb("cvec_sb", [128, 8], F32)
        t_outb = sb("outb_sb", [128, DC], F32)
        t_ones_f = sb("ones_f", [128, 1], F32)
        t_ones_b = sb("ones_b", [128, 1], BF16)
        t_ones_r = sb("ones_r", [1, 128], F32)
        t_stats = sb("stats_sb", [1, 6 * TTW], F32)
        t_scrF = sb("scrF", [128, TTW], F32)
        t_sq = sb("sqscr", [128, 2 * TTW], F32)
        t_tA = sb("tA", [128, NT], BF16)
        t_tB = sb("tB", [128, NT], BF16)
        t_gbf = sb("g_bf", [128, NT], BF16)
        t_f1bf = sb("f1_bf", [128, NT], BF16)
        t_ffnbf = sb("ffn_bf", [128, 2 * TTW], BF16)

        PB = [es_root.enter_context(nc.psum_tensor(f"pb{i}", [128, 512], F32))
              for i in range(8)]
        b.state["bank"] = {i: None for i in range(8)}
        b.state["cw"] = {0: 0, 1: 0}

        class _BG:
            def __getitem__(self, i):
                return b.state["bank"][i]

            def __setitem__(self, i, v):
                b.state["bank"][i] = list(v) if v is not None else None

        bank_guard = _BG()

        class _CW:
            def __getitem__(self, i):
                return b.state["cw"][i]

            def __setitem__(self, i, v):
                b.state["cw"][i] = v

        cw_count_proxy = _CW()

        def guard_waits(banks):
            w = []
            for i in banks:
                g = b.state["bank"][i]
                if g is not None:
                    w.append(tuple(g))
            return w

        def p_preload(t_X):
            b.op("sync", lambda e: e.dma_start(out=t_cvec[:, :], in_=d_cvec.ap()), inc=("sLX", 16))
            b.op("sync", lambda e: e.dma_start(out=t_outb[:, :], in_=d_outb.ap()), inc=("sLX", 16))
            hDC = DC // 2
            b.op("sync", lambda e: e.dma_start(
                out=t_X[:, 0:hDC * NT].rearrange("p (c n) -> p c n", c=hDC),
                in_=d_x.ap()[:, 0:hDC]),
                inc=("sLX", 16))
            b.op("scalar", lambda e: e.dma_start(
                out=t_X[:, hDC * NT:DC * NT].rearrange("p (c n) -> p c n", c=hDC),
                in_=d_x.ap()[:, hDC:DC]),
                inc=("sLX", 16), drain=False)
            b.op("vector", lambda e: e.memset(t_ones_f[:, :], 1.0), inc=("sVE", 1))
            b.op("vector", lambda e: e.memset(t_ones_b[:, :], 1.0), inc=("sVE", 1))
            b.op("vector", lambda e: e.memset(t_ones_r[:, :], 1.0), inc=("sVE", 1))

        def p_ln(t_X, x_ready_waits, t_out):
            apply_done = []
            for T in range(NTT):
                bk = 4 * (T % 2)
                pS1, pS2, pMB, pIB = PB[bk], PB[bk + 1], PB[bk + 2], PB[bk + 3]

                def st(k):
                    return t_stats[:, k * TTW:(k + 1) * TTW]

                s2done = []
                for dc in range(DC):
                    xs = t_X[:, dc * NT + T * TTW: dc * NT + T * TTW + TTW]
                    sqs = t_sq[:, (dc % 2) * TTW:(dc % 2) * TTW + TTW]
                    wsq = list(x_ready_waits) if dc < 2 else [("sPE", s2done[dc - 2])]
                    b.op("scalar", lambda e, xs=xs, sqs=sqs: e.square(sqs, xs),
                         waits=wsq, inc=("sSE", 1), drain=False)
                    b.op("tensor", lambda e, xs=xs, pS1=pS1, dc=dc: e.matmul(
                        pS1[0:1, 0:TTW], t_ones_f[:, :], xs,
                        start=(dc == 0), stop=(dc == DC - 1)),
                        waits=(list(x_ready_waits) + guard_waits([bk])) if dc == 0 else (),
                        inc=("sPE", 1) if dc == DC - 1 else None)
                    b.op("tensor", lambda e, sqs=sqs, pS2=pS2, dc=dc: e.matmul(
                        pS2[0:1, 0:TTW], t_ones_f[:, :], sqs,
                        start=(dc == 0), stop=(dc == DC - 1)),
                        waits=[("sSE", b.v("sSE"))] + (guard_waits([bk + 1]) if dc == 0 else []),
                        inc=("sPE", 1))
                    s2done.append(b.v("sPE"))
                b.op("vector", lambda e, pS1=pS1: e.tensor_scalar(
                    st(0), pS1[0:1, 0:TTW], 1.0 / c.D, None, ALU.mult),
                    waits=[("sPE", b.v("sPE"))], inc=("sVE", 1))
                b.op("vector", lambda e, pS2=pS2: e.tensor_scalar(
                    st(1), pS2[0:1, 0:TTW], 1.0 / c.D, None, ALU.mult), inc=("sVE", 1))
                bank_guard[bk] = ("sVE", b.v("sVE"))
                bank_guard[bk + 1] = ("sVE", b.v("sVE"))
                b.op("vector", lambda e: e.tensor_tensor(st(2), st(0), st(0), ALU.mult), inc=("sVE", 1))
                b.op("vector", lambda e: e.tensor_tensor(st(2), st(1), st(2), ALU.subtract), inc=("sVE", 1))
                b.op("vector", lambda e: e.tensor_scalar(st(2), st(2), EPS, None, ALU.add), inc=("sVE", 1))
                b.op("scalar", lambda e: e.activation(st(3), st(2), AF.Sqrt),
                     waits=[("sVE", b.v("sVE"))], inc=("sSE", 1), drain=False)
                b.op("vector", lambda e: e.reciprocal(st(4), st(3)),
                     waits=[("sSE", b.v("sSE"))], inc=("sVE", 1))
                for _ in range(2):
                    b.op("vector", lambda e: e.tensor_tensor(st(5), st(4), st(4), ALU.mult), inc=("sVE", 1))
                    b.op("vector", lambda e: e.tensor_tensor(st(5), st(5), st(2), ALU.mult), inc=("sVE", 1))
                    b.op("vector", lambda e: e.tensor_scalar(st(5), st(5), -0.5, 1.5, ALU.mult, ALU.add), inc=("sVE", 1))
                    b.op("vector", lambda e: e.tensor_tensor(st(4), st(4), st(5), ALU.mult), inc=("sVE", 1))
                b.op("tensor", lambda e, pMB=pMB: e.matmul(
                    pMB[:, 0:TTW], t_ones_r[0:1, :], st(0), start=True, stop=True),
                    waits=[("sVE", b.v("sVE"))] + guard_waits([bk + 2]), inc=("sPE", 1))
                b.op("tensor", lambda e, pIB=pIB: e.matmul(
                    pIB[:, 0:TTW], t_ones_r[0:1, :], st(4), start=True, stop=True),
                    waits=guard_waits([bk + 3]), inc=("sPE", 1))
                for dc in range(DC):
                    xs = t_X[:, dc * NT + T * TTW: dc * NT + T * TTW + TTW]
                    outs = t_out[:, dc * NT + T * TTW: dc * NT + T * TTW + TTW]
                    b.op("vector", lambda e, xs=xs, pMB=pMB: e.tensor_tensor(
                        t_scrF[:, :], xs, pMB[:, 0:TTW], ALU.subtract),
                        waits=[("sPE", b.v("sPE"))] if dc == 0 else (), inc=("sVE", 1))
                    b.op("vector", lambda e, outs=outs, pIB=pIB: e.tensor_tensor(
                        outs, t_scrF[:, :], pIB[:, 0:TTW], ALU.mult), inc=("sVE", 1))
                bank_guard[bk + 2] = ("sVE", b.v("sVE"))
                bank_guard[bk + 3] = ("sVE", b.v("sVE"))
                apply_done.append(b.v("sVE"))
            return apply_done

        cw_count = cw_count_proxy

        def p_ffn(t_X, t_xhat, xhat_ready, t_act, wgbuf, wlbuf, cache_write,
                  space_ready=()):
            sfx = "" if cache_write else "b"
            fill_val = {}
            pe_done_h = {}
            ve_done_h = {}
            prev_cast_se = {"v": 0}

            def fill_wg(hf, wts):
                bufp = hf % 2
                if cache_write:
                    if cw_count[bufp] > 0:
                        wts = wts + [(f"sCW{bufp}", 16 * cw_count[bufp])]
                    b.op("gpsimd", lambda e, hf=hf, bufp=bufp: e.dma_start(
                        out=wgbuf[bufp][:, :], in_=d_wg.ap()[hf]),
                        waits=wts, inc=(f"sWG{bufp}{sfx}", 16))
                else:
                    b.op("sync", lambda e, hf=hf, bufp=bufp: e.dma_start(
                        out=wgbuf[bufp][:, :], in_=d_wg_e4.ap()[hf]),
                        waits=wts, inc=(f"sWG{bufp}{sfx}", 16))
                fill_val[hf] = (f"sWG{bufp}{sfx}", b.v(f"sWG{bufp}{sfx}"))
                if cache_write:
                    b.op("sync", lambda e, hf=hf, bufp=bufp: e.dma_start(
                        out=d_wg_e4.ap()[hf], in_=wgbuf[bufp][:, :]),
                        waits=[fill_val[hf]],
                        inc=(f"sCW{bufp}", 16))
                    cw_count[bufp] += 1

            for hf in range(min(2, HC)):
                fill_wg(hf, list(space_ready))
            for h in range(HC):
                bufp = h % 2
                bb = 4 * (h % 2)
                first_w = [fill_val[h]]
                if h >= 2:
                    first_w.append(("sVE", ve_done_h[h - 2]))
                first_w += guard_waits([bb + i for i in range(2 * NTT)])
                first = True
                firstT = [True] * NTT
                for gate in range(2):
                    for j in range(DC // 2):
                        base = (gate * DC + 2 * j) * 128
                        lh = wgbuf[bufp][:, base:base + 256].rearrange("p (i m) -> p i m", i=2)
                        xsl = t_xhat[:, 2 * j * NT:(2 * j + 2) * NT].rearrange(
                            "p (i n) -> p i n", i=2)
                        for T in range(NTT):
                            pb = PB[bb + gate * NTT + T]
                            rhs = xsl[:, :, T * TTW:T * TTW + TTW]
                            stop = (j == DC // 2 - 1)
                            w_ = list(first_w) if first else []
                            if firstT[T]:
                                w_.append(("sVE", xhat_ready[T]))
                                firstT[T] = False
                            b.op("tensor", lambda e, pb=pb, lh=lh, rhs=rhs, j=j, stop=stop: e.matmul(
                                pb[:, 0:TTW], lh, rhs, start=(j == 0), stop=stop,
                                perf_mode=DRMODE),
                                waits=w_,
                                inc=("sPE", 1) if stop else None)
                            first = False
                pe_done_h[h] = b.v("sPE")
                if h + 2 < HC:
                    fill_wg(h + 2, [("sPE", pe_done_h[h])])
                wv = [("sPE", b.v("sPE"))]
                for T in range(NTT):
                    pbg = PB[bb + T]
                    pbf = PB[bb + NTT + T]
                    b.op("vector", lambda e, T=T, pbg=pbg: e.tensor_copy(
                        t_gbf[:, T * TTW:T * TTW + TTW], pbg[:, 0:TTW]),
                        waits=wv if T == 0 else (), inc=("sVE", 1))
                    b.op("scalar", lambda e, T=T, pbf=pbf: e.activation(
                        t_f1bf[:, T * TTW:T * TTW + TTW], pbf[:, 0:TTW], AF.Identity),
                        waits=(wv + [("sVE", b.v("sVE"))]) if T == 0 else (),
                        inc=("sSE", 1), drain=False)
                ve_done_h[h] = b.v("sVE")
                se_copy_h = b.v("sSE")
                for i in range(2 * NTT):
                    bank_guard[bb + i] = ("sVE", ve_done_h[h]) if i < NTT else ("sSE", se_copy_h)
                b.op("scalar", lambda e: e.square(t_tA[:, :], t_gbf[:, :]),
                     waits=[("sVE", b.v("sVE")), ("sSE", prev_cast_se["v"])],
                     inc=("sSE", 1), drain=False)
                b.op("vector", lambda e: e.tensor_tensor(t_tB[:, :], t_tA[:, :], t_gbf[:, :], ALU.mult),
                     waits=[("sSE", b.v("sSE"))], inc=("sVE", 1), drain=False)
                b.op("vector", lambda e: e.tensor_scalar(
                    t_tA[:, :], t_tB[:, :], C1, None, ALU.mult), inc=("sVE", 1))
                b.op("vector", lambda e: e.tensor_tensor(
                    t_tB[:, :], t_gbf[:, :], t_tA[:, :], ALU.add), inc=("sVE", 1))
                b.op("vector", lambda e: e.tensor_scalar(
                    t_tA[:, :], t_tB[:, :], C2, None, ALU.mult), inc=("sVE", 1))
                b.op("scalar", lambda e: e.activation(t_tB[:, :], t_tA[:, :], AF.Tanh),
                     waits=[("sVE", b.v("sVE"))], inc=("sSE", 1), drain=False)
                b.op("vector", lambda e: e.tensor_scalar(
                    t_tA[:, :], t_tB[:, :], 1.0, None, ALU.add),
                    waits=[("sSE", b.v("sSE"))], inc=("sVE", 1), drain=False)
                b.op("vector", lambda e: e.scalar_tensor_tensor(
                    t_tB[:, :], t_tA[:, :], 0.5, t_gbf[:, :], ALU.mult, ALU.mult), inc=("sVE", 1))
                b.op("vector", lambda e: e.tensor_tensor(
                    t_tA[:, :], t_tB[:, :], t_f1bf[:, :], ALU.mult),
                    waits=[("sSE", se_copy_h)], inc=("sVE", 1))
                b.op("scalar", lambda e, h=h: e.copy(
                    t_act[:, h * NT:(h + 1) * NT], t_tA[:, :]),
                    waits=[("sVE", b.v("sVE"))], inc=("sSE", 1), drain=False)
                prev_cast_se["v"] = b.v("sSE")
            act_ready = ("sSE", b.v("sSE"))

            fill3_val = {}
            pe_done_oc = {}
            ve_done_oc = {}

            def fill_wl(of, wts):
                bufp = of % 2
                if cache_write:
                    if cw_count[bufp] > 0:
                        wts = wts + [(f"sCW{bufp}", 16 * cw_count[bufp])]
                    b.op("gpsimd", lambda e, of=of, bufp=bufp: e.dma_start(
                        out=wlbuf[bufp][:, :], in_=d_wlin.ap()[of]),
                        waits=wts, inc=(f"sWL{bufp}{sfx}", 16))
                else:
                    b.op("sync", lambda e, of=of, bufp=bufp: e.dma_start(
                        out=wlbuf[bufp][:, :], in_=d_wlin_e4.ap()[of]),
                        waits=wts, inc=(f"sWL{bufp}{sfx}", 16))
                fill3_val[of] = (f"sWL{bufp}{sfx}", b.v(f"sWL{bufp}{sfx}"))
                if cache_write:
                    b.op("sync", lambda e, of=of, bufp=bufp: e.dma_start(
                        out=d_wlin_e4.ap()[of], in_=wlbuf[bufp][:, :]),
                        waits=[fill3_val[of]],
                        inc=(f"sCW{bufp}", 16))
                    cw_count[bufp] += 1

            for of in range(min(2, DC)):
                gate_done = [("sPE", pe_done_h[HC - 2 + (of % 2)])]
                fill_wl(of, list(space_ready) + gate_done)
            for oc in range(DC):
                bufp = oc % 2
                banks = [2 * (oc % 2) + T for T in range(NTT)]
                first_w = [fill3_val[oc], act_ready]
                if oc >= 2:
                    first_w.append(("sVE", ve_done_oc[oc - 2]))
                first_w += guard_waits(banks)
                first = True
                for jc in range(HC // 2):
                    lh = wlbuf[bufp][:, 2 * jc * 128:2 * jc * 128 + 256].rearrange(
                        "p (i m) -> p i m", i=2)
                    asl = t_act[:, 2 * jc * NT:(2 * jc + 2) * NT].rearrange(
                        "p (i n) -> p i n", i=2)
                    for T in range(NTT):
                        pb = PB[banks[T]]
                        rhs = asl[:, :, T * TTW:T * TTW + TTW]
                        stop = (jc == HC // 2 - 1)
                        b.op("tensor", lambda e, pb=pb, lh=lh, rhs=rhs, jc=jc, stop=stop: e.matmul(
                            pb[:, 0:TTW], lh, rhs, start=(jc == 0), stop=stop,
                            perf_mode=DRMODE),
                            waits=first_w if first else (),
                            inc=("sPE", 1) if stop else None)
                        first = False
                pe_done_oc[oc] = b.v("sPE")
                if oc + 2 < DC:
                    fill_wl(oc + 2, [("sPE", pe_done_oc[oc])])
                wv = [("sPE", b.v("sPE"))]
                for T in range(NTT):
                    pb = PB[banks[T]]
                    fb = t_ffnbf[:, T * TTW:T * TTW + TTW]
                    xs = t_X[:, oc * NT + T * TTW: oc * NT + T * TTW + TTW]
                    b.op("vector", lambda e, pb=pb, fb=fb: e.tensor_copy(fb, pb[:, 0:TTW]),
                         waits=wv if T == 0 else (), inc=("sVE", 1))
                    b.op("vector", lambda e, fb=fb, xs=xs: e.scalar_tensor_tensor(
                        xs, fb, 0.5, xs, ALU.mult, ALU.add), inc=("sVE", 1))
                ve_done_oc[oc] = b.v("sVE")
                for i in banks:
                    bank_guard[i] = ("sVE", ve_done_oc[oc])
            return ve_done_oc

        esX = ExitStack()
        t_X = esX.enter_context(nc.sbuf_tensor("X1", [128, DC * NT], F32, side="right"))

        es1 = ExitStack()
        t_xhat_e4 = es1.enter_context(nc.sbuf_tensor("xhat_e4", [128, DC * NT], E4))
        t_act = es1.enter_context(nc.sbuf_tensor("act_e4", [128, HC * NT], E4))
        wbufs = [es1.enter_context(nc.sbuf_tensor(f"wbuf{i}", [128, max(2 * DC, HC) * 128], E4)) for i in range(2)]
        wgbuf = [t[:, 0:2 * DC * 128] for t in wbufs]
        wlbuf = [t[:, 0:HC * 128] for t in wbufs]

        ln1_ve = [None]
        b.phase("blk1", [
            lambda: p_preload(t_X),
            lambda: ln1_ve.__setitem__(0, p_ln(
                t_X, [("sLX", b.v("sLX")), ("sVE", b.v("sVE"))], t_xhat_e4)),
            lambda: p_ffn(t_X, t_xhat_e4, ln1_ve[0], t_act, wgbuf, wlbuf, True),
        ])
        

        es1.close()

        esQ = ExitStack()
        t_Qd = esQ.enter_context(nc.sbuf_tensor("Qd", [128, RC * NT], BF16))
        t_wqu_b = [esQ.enter_context(nc.sbuf_tensor(f"wqu_b{i}", [128, c.R], BF16)) for i in range(2)]
        t_wku_b = [esQ.enter_context(nc.sbuf_tensor(f"wku_b{i}", [128, c.R], BF16)) for i in range(2)]
        t_wvu2 = esQ.enter_context(nc.sbuf_tensor("wvu2", [128, RC * 2 * 128], BF16))
        es2 = ExitStack()
        t_xhat_bf = es2.enter_context(nc.sbuf_tensor("xhat_bf", [128, DC * NT], BF16))
        t_wqkv = [es2.enter_context(nc.sbuf_tensor(f"wqkv_sb{i}", [128, DC * RC * 128], BF16))
                  for i in range(2)]
        t_Kd = es2.enter_context(nc.sbuf_tensor("Kd_loc", [128, RC * NT], BF16))
        t_Vd = es2.enter_context(nc.sbuf_tensor("Vd_loc", [128, RC * NT], BF16))

        resid1 = b.v("sVE")
        cw_final = [("sCW0", b.v("sCW0")), ("sCW1", b.v("sCW1"))]
        ln2_ve = [None]

        wu_val0 = {}

        def phase_qkv():
            pe_proj_done = {}
            wfill = {}

            def fill_wqkv(p, wts):
                par = p % 2
                view = d_wqkv.ap()[:, p * DC * RC * 128:(p + 1) * DC * RC * 128]
                b.op("gpsimd", lambda e, par=par, view=view: e.dma_start(
                    out=t_wqkv[par][:, :], in_=view), waits=wts, inc=(f"sWA{par}", 16))
                wfill[p] = (f"sWA{par}", b.v(f"sWA{par}"))

            fill_wqkv(0, cw_final)
            fill_wqkv(1, [])
            hDC = DC // 2
            b.op("sync", lambda e: e.dma_start(
                out=d_xspill.ap()[:, 0:hDC],
                in_=t_X[:, 0:hDC * NT].rearrange("p (c n) -> p c n", c=hDC)),
                waits=[("sVE", ln2_ve[0][-1])], inc=("sSP", 16))
            b.op("scalar", lambda e: e.dma_start(
                out=d_xspill.ap()[:, hDC:DC],
                in_=t_X[:, hDC * NT:DC * NT].rearrange("p (c n) -> p c n", c=hDC)),
                waits=[("sVE", ln2_ve[0][-1])], inc=("sSP", 16), drain=False)
            outs = {0: t_Kd, 1: t_Vd, 2: t_Qd}
            kd_vd_done = [0]
            for proj in range(3):
                for rc in range(RC):
                    for T in range(NTT):
                        k = (proj * RC * NTT + rc * NTT + T) % 4
                        pb = PB[k]
                        first_w = [wfill[proj], ("sVE", ln2_ve[0][T])] + guard_waits([k])
                        for dc in range(DC):
                            lh = t_wqkv[proj % 2][:, (dc * RC + rc) * 128:
                                                  (dc * RC + rc) * 128 + 128]
                            rhs = t_xhat_bf[:, dc * NT + T * TTW: dc * NT + T * TTW + TTW]
                            stop = (dc == DC - 1)
                            b.op("tensor", lambda e, pb=pb, lh=lh, rhs=rhs, dc=dc, stop=stop: e.matmul(
                                pb[:, 0:TTW], lh, rhs, start=(dc == 0), stop=stop),
                                waits=first_w if dc == 0 else (),
                                inc=("sPE", 1) if stop else None)
                        dst = outs[proj][:, rc * NT + T * TTW: rc * NT + T * TTW + TTW]
                        b.op("vector", lambda e, dst=dst, pb=pb: e.tensor_copy(dst, pb[:, 0:TTW]),
                             waits=[("sPE", b.v("sPE"))], inc=("sVE", 1), drain=False)
                        bank_guard[k] = ("sVE", b.v("sVE"))
                pe_proj_done[proj] = b.v("sPE")
                if proj + 2 < 3:
                    fill_wqkv(proj + 2, [("sPE", pe_proj_done[proj])])
                if proj == 0:
                    b.op("sync", lambda e: e.dma_start(
                        out=d_ccin_k.transpose([1, 0, 2]),
                        in_=t_Kd[:, :].rearrange("p (rc t) -> p rc t", rc=RC)),
                        waits=[("sVE", b.v("sVE"))], inc=("sCIK", 16))
                if proj == 1:
                    kd_vd_done[0] = b.v("sVE")
            b.op("sync", lambda e: e.dma_start(
                out=d_ccin_v.transpose([1, 0, 2]),
                in_=t_Vd[:, :].rearrange("p (rc t) -> p rc t", rc=RC)),
                waits=[("sVE", kd_vd_done[0])], inc=("sCIN", 16))

        b.phase("blk2", [
            lambda: ln2_ve.__setitem__(
                0, p_ln(t_X, [("sVE", resid1)] + cw_final, t_xhat_bf)),
            phase_qkv,
        ])
        es2.close()
        esX.close()

        esC = ExitStack()
        t_ctx = esC.enter_context(nc.sbuf_tensor("ctx", [128, H * NT], BF16))
        es5 = ExitStack()
        t_mask = es5.enter_context(nc.sbuf_tensor("mask_sb", [128, G * c.MW], BF16))
        t_binv = es5.enter_context(nc.sbuf_tensor("binv", [128, TTW], F32))
        t_Kda = es5.enter_context(nc.sbuf_tensor("Kd_all", [128, RC * c.S], E4))
        t_Vda = es5.enter_context(nc.sbuf_tensor("Vd_all", [128, RC * c.S], BF16))
        t_Qm = [es5.enter_context(nc.sbuf_tensor(f"Qm{i}", [128, RC * NT], E4)) for i in range(2)]
        t_M = [es5.enter_context(nc.sbuf_tensor(f"M_sb{i}", [128, RC * c.R], BF16)) for i in range(2)]
        t_Vup = es5.enter_context(nc.sbuf_tensor("Vup", [128, c.SBLK * 2 * 128], BF16))
        t_E = es5.enter_context(nc.sbuf_tensor("E_sb", [128, 4 * TTW], BF16))


        def phase_kvload():
            for hh in range(min(2, H)):
                par = hh % 2
                wts = [("sCIK", 16), ("sCIN", 16), ("sSP", 32)] if hh == 0 else []
                b.op("gpsimd", lambda e, hh=hh, par=par: e.dma_start(
                    out=t_wqu_b[par][:, :], in_=d_wqu.ap()[hh]), waits=wts, inc=(f"sWU{par}", 16))
                b.op("gpsimd", lambda e, hh=hh, par=par: e.dma_start(
                    out=t_wku_b[par][:, :], in_=d_wku.ap()[hh]), inc=(f"sWU{par}", 16))
                view = t_wvu2[:, :].rearrange("p (rc q n) -> p rc q n", q=2, n=128)[:, :, par]
                b.op("gpsimd", lambda e, hh=hh, view=view: e.dma_start(
                    out=view, in_=d_wvu.ap()[hh].rearrange("p (rc n) -> p rc n", n=128)),
                    inc=(f"sWU{par}", 16))
                wu_val0[hh] = (f"sWU{par}", b.v(f"sWU{par}"))
            rgs = [[g * G + i for i in range(G)] for g in range(c.NG)]
            b.op("gpsimd", lambda e: e.collective_compute(
                "AllGather", ALU.bypass, replica_groups=rgs,
                ins=[d_ccin.ap().opt()], outs=[d_ccout.ap().opt()]),
                waits=[("sCIK", 16), ("sCIN", 16)], inc=("sCC", 1))
            b.op("sync", lambda e: e.dma_start(
                out=t_mask[:, :].rearrange("p (g w) -> p g w", g=G), in_=d_mask.ap()),
                waits=[("sSP", 32), ("sCIK", 16)], inc=("sKV", 16))
            for kv, dst, sem, engq in ((0, t_Kda, "sKV8", "gpsimd"), (1, t_Vda, "sKV2", "scalar")):
                for gi in range(G):
                    view = dst[:, :].rearrange("p (rc g t) -> p rc g t", rc=RC, g=G)[:, :, gi]
                    b.op(engq, lambda e, view=view, gi=gi, kv=kv: e.dma_start(
                        out=view, in_=d_ccout.ap()[gi, kv].transpose([1, 0, 2])),
                        waits=[("sCC", 1)] if gi == 0 else (),
                        inc=(sem, 16), drain=False)


        def phase_heads():
            ctx_all = [0]
            wu_val = {}
            qm_ready = {}
            vup_ready = [0]
            pending_fin = []
            b.state["lg"] = 0

            def emit_loads(hh, wts):
                par = hh % 2
                b.op("gpsimd", lambda e, hh=hh, par=par: e.dma_start(
                    out=t_wqu_b[par][:, :], in_=d_wqu.ap()[hh]), waits=wts, inc=(f"sWU{par}", 16))
                b.op("gpsimd", lambda e, hh=hh, par=par: e.dma_start(
                    out=t_wku_b[par][:, :], in_=d_wku.ap()[hh]), inc=(f"sWU{par}", 16))
                view = t_wvu2[:, :].rearrange("p (rc q n) -> p rc q n", q=2, n=128)[:, :, par]
                b.op("gpsimd", lambda e, hh=hh, view=view: e.dma_start(
                    out=view, in_=d_wvu.ap()[hh].rearrange("p (rc n) -> p rc n", n=128)),
                    inc=(f"sWU{par}", 16))
                wu_val[hh] = (f"sWU{par}", b.v(f"sWU{par}"))

            def emit_mqm(hh):
                par = hh % 2
                for qc in range(RC):
                    pb = PB[qc % 2]
                    b.op("tensor", lambda e, pb=pb, qc=qc, par=par: e.matmul(
                        pb[:, 0:c.R], t_wqu_b[par][:, qc * 128:qc * 128 + 128],
                        t_wku_b[par][:, :], start=True, stop=True),
                        waits=([wu_val[hh]] if qc == 0 else []) + guard_waits([qc % 2]),
                        inc=("sPE", 1))
                    dst = t_M[par][:, qc * c.R:(qc + 1) * c.R]
                    b.op("vector", lambda e, dst=dst, pb=pb: e.tensor_scalar(
                        dst, pb[:, 0:c.R], RSQ_DH, None, ALU.mult),
                        waits=[("sPE", b.v("sPE"))], inc=("sVE", 1), drain=False)
                    bank_guard[qc % 2] = ("sVE", b.v("sVE"))
                m_ready = b.v("sVE")
                for rk in range(RC):
                    for T in range(NTT):
                        k = (rk * NTT + T) % 4
                        pb = PB[k]
                        first_w = [("sVE", m_ready)] + guard_waits([k])
                        for qc in range(RC):
                            lh = t_M[par][:, qc * c.R + rk * 128: qc * c.R + rk * 128 + 128]
                            rhs = t_Qd[:, qc * NT + T * TTW: qc * NT + T * TTW + TTW]
                            stop = (qc == RC - 1)
                            b.op("tensor", lambda e, pb=pb, lh=lh, rhs=rhs, qc=qc, stop=stop: e.matmul(
                                pb[:, 0:TTW], lh, rhs, start=(qc == 0), stop=stop),
                                waits=first_w if qc == 0 else (),
                                inc=("sPE", 1) if stop else None)
                        dst = t_Qm[par][:, rk * NT + T * TTW: rk * NT + T * TTW + TTW]
                        b.op("vector", lambda e, dst=dst, pb=pb: e.tensor_scalar(
                            dst, pb[:, 0:TTW], 16.0, None, ALU.mult),
                             waits=[("sPE", b.v("sPE"))], inc=("sVE", 1), drain=False)
                        bank_guard[k] = ("sVE", b.v("sVE"))
                qm_ready[par] = b.v("sVE")

            wu_val.update(wu_val0)
            emit_mqm(0)
            if H > 1:
                emit_mqm(1)

            for h in range(H):
                par = h % 2
                if h % 2 == 0:
                    for sbk in range(c.SBLK):
                        pb = PB[2 + (sbk % 2)]
                        first_w = ([("sKV2", 16 * G)] if (h == 0 and sbk == 0) else []) + \
                            ([wu_val[h], wu_val.get(h + 1, wu_val[h])] if sbk == 0 else []) + \
                            guard_waits([2 + (sbk % 2)])
                        for rc in range(RC):
                            lh = t_Vda[:, rc * c.S + sbk * 128: rc * c.S + sbk * 128 + 128]
                            rhs = t_wvu2[:, rc * 256:rc * 256 + 256]
                            stop = (rc == RC - 1)
                            b.op("tensor", lambda e, pb=pb, lh=lh, rhs=rhs, rc=rc, stop=stop: e.matmul(
                                pb[:, 0:256], lh, rhs, start=(rc == 0), stop=stop),
                                waits=first_w if rc == 0 else (),
                                inc=("sPE", 1) if stop else None)
                        dst = t_Vup[:, sbk * 256:(sbk + 1) * 256]
                        b.op("vector", lambda e, dst=dst, pb=pb: e.tensor_copy(dst, pb[:, 0:256]),
                             waits=[("sPE", b.v("sPE"))], inc=("sVE", 1), drain=False)
                        bank_guard[2 + (sbk % 2)] = ("sVE", b.v("sVE"))
                    vup_ready[0] = b.v("sVE")
                    if h + 2 < H:
                        emit_loads(h + 2, [("sPE", b.v("sPE"))])
                    if h + 3 < H:
                        emit_loads(h + 3, [("sPE", b.v("sPE"))])
                for T in range(NTT):
                    pbC = PB[4 + (T % 2)]
                    pbL = PB[6 + (T % 2)]
                    blocks = [(r, m) for m in range(c.BND * (T + 1)) for r in range(G)]
                    n_bl = len(blocks)
                    exp_done = {}
                    edep = {}
                    ctxsum_done = {}

                    lgslot = {}

                    def emit_logits(i):
                        r, m = blocks[i]
                        lgslot[i] = b.state["lg"] % 4
                        b.state["lg"] += 1
                        pbLg = PB[lgslot[i]]
                        first_w = (([("sVE", qm_ready[par]), ("sVE", vup_ready[0])] +
                                    ([("sKV", 16), ("sKV8", 16 * G)] if (h == 0 and T == 0) else []))
                                   if i == 0 else []) + guard_waits([lgslot[i]])
                        kda3 = t_Kda[:, :].rearrange("p (rc s) -> p rc s", rc=RC)
                        qm3 = t_Qm[par][:, :].rearrange("p (rc t) -> p rc t", rc=RC)
                        for j in range(RC // 2):
                            lh = kda3[:, 2 * j:2 * j + 2,
                                      r * NT + m * 128: r * NT + m * 128 + 128]
                            rhs = qm3[:, 2 * j:2 * j + 2, T * TTW: T * TTW + TTW]
                            stop = (j == RC // 2 - 1)
                            b.op("tensor", lambda e, pbLg=pbLg, lh=lh, rhs=rhs, j=j, stop=stop: e.matmul(
                                pbLg[:, 0:TTW], lh, rhs, start=(j == 0), stop=stop,
                                perf_mode=DRMODE),
                                waits=first_w if j == 0 else (),
                                inc=("sPE", 1) if stop else None)
                        ew = [("sPE", b.v("sPE"))]
                        if i >= 4:
                            ew.append(("sPE", ctxsum_done[i - 4]))
                        esl = t_E[:, (i % 4) * TTW:(i % 4) * TTW + TTW]
                        b.op("scalar", lambda e, esl=esl, pbLg=pbLg: e.activation(
                            esl, pbLg[:, 0:TTW], AF.Exp, bias=t_cvec[:, 1:2],
                            scale=t_cvec[:, 3:4]),
                            waits=ew, inc=("sSE", 1), drain=False)
                        exp_done[i] = b.v("sSE")
                        bank_guard[lgslot[i]] = ("sSE", b.v("sSE"))
                        masked = (m >= c.BND * T)
                        dep = [("sSE", exp_done[i])]
                        if masked:
                            dlt = m - c.BND * T
                            msl = t_mask[:, r * c.MW + 128 * (c.BND - 1 - dlt):
                                         r * c.MW + 128 * (c.BND - 1 - dlt) + TTW]
                            b.op("vector", lambda e, esl=esl, msl=msl: e.tensor_tensor(
                                esl, esl, msl, ALU.mult),
                                waits=[("sSE", b.v("sSE"))], inc=("sVE", 1), drain=False)
                            dep.append(("sVE", b.v("sVE")))
                        edep[i] = (esl, dep)

                    def emit_ctxsum(i):
                        r, m = blocks[i]
                        esl, dep = edep[i]
                        sbid = r * c.NTB + m
                        cw = dep[-1:]
                        if i == 0:
                            cw = list(dep) + guard_waits([4 + (T % 2), 6 + (T % 2)])
                        lhv = t_Vup[:, sbid * 256 + par * 128: sbid * 256 + par * 128 + 128]
                        b.op("tensor", lambda e, pbC=pbC, lhv=lhv, esl=esl, i=i: e.matmul(
                            pbC[:, 0:TTW], lhv, esl, start=(i == 0), stop=(i == n_bl - 1)),
                            waits=cw)
                        b.op("tensor", lambda e, esl=esl, i=i: e.matmul(
                            pbL[0:1, 0:TTW], t_ones_b[:, :], esl,
                            start=(i == 0), stop=(i == n_bl - 1)),
                            inc=("sPE", 1))
                        ctxsum_done[i] = b.v("sPE")

                    for i in range(min(3, n_bl)):
                        emit_logits(i)
                    if pending_fin:
                        pending_fin.pop(0)()
                    for i in range(n_bl):
                        if i + 3 < n_bl:
                            emit_logits(i + 3)
                        emit_ctxsum(i)

                    def finalize(h=h, T=T, pbC=pbC, pbL=pbL,
                                 pe_after=b.v("sPE")):
                        def stl(k):
                            return t_stats[:, k * TTW:(k + 1) * TTW]
                        b.op("vector", lambda e: e.reciprocal(stl(0), pbL[0:1, 0:TTW]),
                             waits=[("sPE", pe_after)], inc=("sVE", 1), drain=False)
                        bank_guard[6 + (T % 2)] = ("sVE", b.v("sVE"))
                        slot = b.state["lg"] % 4
                        b.state["lg"] += 1
                        b.op("tensor", lambda e, slot=slot: e.matmul(
                            PB[slot][:, 0:TTW], t_ones_r[0:1, :], stl(0), start=True, stop=True),
                            waits=[("sVE", b.v("sVE"))] + guard_waits([slot]), inc=("sPE", 1))
                        b.op("vector", lambda e, slot=slot: e.tensor_copy(t_binv[:, :], PB[slot][:, 0:TTW]),
                             waits=[("sPE", b.v("sPE"))], inc=("sVE", 1))
                        bank_guard[slot] = ("sVE", b.v("sVE"))
                        dst = t_ctx[:, h * NT + T * TTW: h * NT + T * TTW + TTW]
                        b.op("vector", lambda e, dst=dst: e.tensor_tensor(
                            dst, pbC[:, 0:TTW], t_binv[:, :], ALU.mult), inc=("sVE", 1))
                        bank_guard[4 + (T % 2)] = ("sVE", b.v("sVE"))

                    pending_fin.append(finalize)
                if h + 2 < H:
                    emit_mqm(h + 2)
            while pending_fin:
                pending_fin.pop(0)()
            ctx_all[0] = b.v("sVE")
            return ctx_all[0]

        ctx_ready = [None]
        b.phase("blk3", [phase_kvload,
                         lambda: ctx_ready.__setitem__(0, phase_heads())])
        es5.close()

        es6 = ExitStack()
        t_X2 = es6.enter_context(nc.sbuf_tensor("X2", [128, DC * NT], F32, side="right"))
        esA = ExitStack()
        t_attnscr = esA.enter_context(nc.sbuf_tensor("attnscr", [128, 2 * TTW], F32))
        t_wo_b = [esA.enter_context(nc.sbuf_tensor(f"wo_b{i}", [128, H * 128], BF16)) for i in range(2)]

        def phase_outproj():
            hDC = DC // 2
            b.op("sync", lambda e: e.dma_start(
                out=t_X2[:, 0:hDC * NT].rearrange("p (c n) -> p c n", c=hDC),
                in_=d_xspill.ap()[:, 0:hDC]),
                waits=[("sSP", 32)], inc=("sSP", 16))
            b.op("scalar", lambda e: e.dma_start(
                out=t_X2[:, hDC * NT:DC * NT].rearrange("p (c n) -> p c n", c=hDC),
                in_=d_xspill.ap()[:, hDC:DC]),
                waits=[("sSP", 32)], inc=("sSP", 16), drain=False)
            fillo_val = {}
            pe_done = {}
            ve_done = {}
            se_done = {}

            def fill_wo(of, wts):
                par = of % 2
                b.op("gpsimd", lambda e, of=of, par=par: e.dma_start(
                    out=t_wo_b[par][:, :], in_=d_wout.ap()[of]),
                    waits=wts, inc=(f"sWO{par}", 16))
                fillo_val[of] = (f"sWO{par}", b.v(f"sWO{par}"))

            for of in range(min(2, DC)):
                fill_wo(of, [])
            for oc in range(DC):
                par = oc % 2
                banks = [2 * (oc % 2) + T for T in range(NTT)]
                first_w = [fillo_val[oc], ("sVE", ctx_ready[0])] + guard_waits(banks)
                first = True
                for hh in range(H):
                    lh = t_wo_b[par][:, hh * 128:hh * 128 + 128]
                    for T in range(NTT):
                        pb = PB[banks[T]]
                        rhs = t_ctx[:, hh * NT + T * TTW: hh * NT + T * TTW + TTW]
                        stop = (hh == H - 1)
                        b.op("tensor", lambda e, pb=pb, lh=lh, rhs=rhs, hh=hh, stop=stop: e.matmul(
                            pb[:, 0:TTW], lh, rhs, start=(hh == 0), stop=stop),
                            waits=first_w if first else (),
                            inc=("sPE", 1) if stop else None)
                        first = False
                pe_done[oc] = b.v("sPE")
                if oc + 2 < DC:
                    fill_wo(oc + 2, [("sPE", pe_done[oc])])
                for T in range(NTT):
                    pb = PB[banks[T]]
                    asl = t_attnscr[:, T * TTW:T * TTW + TTW]
                    xs = t_X2[:, oc * NT + T * TTW: oc * NT + T * TTW + TTW]
                    sw = [("sPE", pe_done[oc])]
                    if oc >= 1:
                        sw.append(("sVE", ve_done[oc - 1]))
                    b.op("scalar", lambda e, asl=asl, pb=pb, oc=oc: e.activation(
                        asl, pb[:, 0:TTW], AF.Identity, bias=t_outb[:, oc:oc + 1]),
                        waits=sw, inc=("sSE", 1), drain=False)
                    vw = [("sSE", b.v("sSE"))]
                    if oc == 0 and T == 0:
                        vw.append(("sSP", 64))
                    b.op("vector", lambda e, asl=asl, xs=xs: e.tensor_tensor(
                        xs, xs, asl, ALU.add), waits=vw, inc=("sVE", 1))
                ve_done[oc] = b.v("sVE")
                se_done[oc] = b.v("sSE")
                for i in banks:
                    bank_guard[i] = ("sSE", se_done[oc])

        b.phase("outproj", phase_outproj)
        esA.close()
        esC.close()

        es7 = ExitStack()
        t_xhat_e4b = es7.enter_context(nc.sbuf_tensor("xhat_e4b", [128, DC * NT], E4))
        t_act2 = es7.enter_context(nc.sbuf_tensor("act_e4b", [128, HC * NT], E4))
        wbufs2 = [es7.enter_context(nc.sbuf_tensor(f"wbuf2_{i}", [128, max(2 * DC, HC) * 128], E4)) for i in range(2)]
        wgbuf2 = [t[:, 0:2 * DC * 128] for t in wbufs2]
        wlbuf2 = [t[:, 0:HC * 128] for t in wbufs2]

        ffn2_oc = [None]

        def phase_store():
            for oc in range(DC):
                b.op("sync", lambda e, oc=oc: e.dma_start(
                    out=d_y.ap()[:, oc],
                    in_=t_X2[:, oc * NT:(oc + 1) * NT]),
                    waits=[("sVE", ffn2_oc[0][oc])] if oc in (0, DC - 1) or True else (),
                    inc=("sY", 16))
            b.op("sync", lambda e: e.nop(), waits=[("sY", 16 * DC)])

        resid2 = b.v("sVE")
        space2 = [("sVE", b.v("sVE")), ("sPE", b.v("sPE")), ("sSE", b.v("sSE"))]
        ln3_ve = [None]
        b.phase("blk5", [
            lambda: ln3_ve.__setitem__(0, p_ln(t_X2, [("sVE", resid2)], t_xhat_e4b)),
            lambda: ffn2_oc.__setitem__(0, p_ffn(
                t_X2, t_xhat_e4b, ln3_ve[0], t_act2, wgbuf2, wlbuf2, False,
                space_ready=space2)),
            phase_store,
        ])

        es7.close()
        es6.close()
        esQ.close()

    return nc



def _shared_weight_layout(cfg, wg, wlin, qd, kd, vd, qu, ku, vu, ow, ob):
    c = cfg
    out = {}
    a = np.ascontiguousarray(
        wg.reshape(2, c.DC // 2, 2, 128, c.HC, 128).transpose(4, 3, 0, 1, 2, 5)
    ).reshape(c.HC, 128, 2 * c.DC * 128)
    out["wg"] = np.ascontiguousarray(a)
    a = np.ascontiguousarray(
        wlin.reshape(c.HC // 2, 2, 128, c.DC, 128).transpose(3, 2, 0, 1, 4)
    ).reshape(c.DC, 128, c.HC * 128)
    out["wlin"] = np.ascontiguousarray(a)
    qkv = []
    for w in (kd, vd, qd):
        t = w.reshape(c.RC, 128, c.DC, 128).transpose(3, 2, 0, 1)
        qkv.append(t)
    out["wqkv"] = np.ascontiguousarray(np.stack(qkv, axis=1)).reshape(128, 3 * c.DC * c.RC * 128)
    out["wqkv"] = np.ascontiguousarray(
        np.stack(qkv, axis=1).transpose(0, 1, 2, 3, 4)).reshape(128, 3 * c.DC * c.RC * 128)
    out["wqu"] = np.ascontiguousarray(qu.reshape(c.H, 128, c.R))
    out["wku"] = np.ascontiguousarray(ku.reshape(c.H, 128, c.R))
    out["wvu"] = np.ascontiguousarray(
        vu.reshape(c.H, 128, c.RC, 128).transpose(0, 3, 2, 1)).reshape(c.H, 128, c.RC * 128)
    out["wout"] = np.ascontiguousarray(
        ow.reshape(c.DC, 128, c.H, 128).transpose(0, 3, 2, 1)).reshape(c.DC, 128, c.H * 128)
    out["outb"] = np.ascontiguousarray(ob.reshape(c.DC, 128).T)
    cvec = np.zeros((128, 8), np.float32)
    cvec[:, 0] = EPS
    cvec[:, 1] = EXP_SHIFT
    cvec[:, 2] = RSQ_DH
    cvec[:, 3] = 0.0625
    out["cvec"] = cvec
    return out


def _mask_for_rank(cfg, r_c):
    c = cfg
    p = np.arange(128)[:, None]
    g = np.arange(c.MW)[None, :]
    mb = np.zeros((128, c.G, c.MW), BFNP)
    for r in range(c.G):
        dlt = 1 if r > r_c else 0
        mb[:, r, :] = (g >= p + 128 * (c.BND - 1) + dlt).astype(BFNP)
    return mb


def _core_x(cfg, xb, r):
    c = cfg
    xloc = xb[r::c.G][:c.NT]
    return np.ascontiguousarray(
        xloc.T.reshape(c.DC, 128, c.NT).transpose(1, 0, 2))


def run(cfg, inputs, core_ids=None, runner=run_bass_kernel_spmd):
    c = cfg
    x = np.asarray(inputs["L0_attn_x"], np.float32)
    shared = _shared_weight_layout(
        c,
        np.asarray(inputs["L0_ffn_w_gating"], np.float32),
        np.asarray(inputs["L0_ffn_w_linear"], np.float32),
        np.asarray(inputs["L0_attn_q_down_w"], np.float32),
        np.asarray(inputs["L0_attn_k_down_w"], np.float32),
        np.asarray(inputs["L0_attn_v_down_w"], np.float32),
        np.asarray(inputs["L0_attn_q_up_w"], np.float32),
        np.asarray(inputs["L0_attn_k_up_w"], np.float32),
        np.asarray(inputs["L0_attn_v_up_w"], np.float32),
        np.asarray(inputs["L0_attn_out_w"], np.float32),
        np.asarray(inputs["L0_attn_out_b"], np.float32),
    )
    masks = [_mask_for_rank(c, r) for r in range(c.G)]
    in_maps = []
    for core in range(c.NC):
        g, r = core // c.G, core % c.G
        m = dict(shared)
        m["x"] = _core_x(c, x[g], r)
        m["maskbig"] = masks[r]
        in_maps.append(m)
    nc = build_program(c)
    res = runner(nc, in_maps, core_ids=core_ids or list(range(c.NC)))
    out = np.empty((c.NG, c.G * c.NT, c.D), np.float32)
    for core in range(c.NC):
        g, r = core // c.G, core % c.G
        Y = np.asarray(res.results[core]["y"]).reshape(128, c.DC, c.NT)
        out[g, r::c.G] = Y.transpose(2, 1, 0).reshape(c.NT, c.D)
    return out


def kernel(**inputs):
    return run(FULL, inputs)


# version 23
# speedup vs baseline: 1.1148x; 1.0065x over previous
import numpy as np
import ml_dtypes
from contextlib import ExitStack

import concourse.bass as bass
import concourse.mybir as mybir
from concourse.bass_utils import run_bass_kernel_spmd

F32 = mybir.dt.float32
F32R = mybir.dt.float32r
BF16 = mybir.dt.bfloat16
E4 = mybir.dt.float8e4
AF = mybir.ActivationFunctionType
ALU = mybir.AluOpType
DRMODE = mybir.MatmulPerfMode.DoubleRow

BFNP = ml_dtypes.bfloat16
E4NP = ml_dtypes.float8_e4m3fn

C1 = float(np.float32(BFNP(0.044715)))
C2 = float(np.float32(BFNP(np.sqrt(2.0 / np.pi))))
EPS = 1e-5
EXP_SHIFT = -2.0
RSQ_DH = float(1.0 / np.sqrt(128.0))


class CFG:
    def __init__(self, G=4, NG=2, D=2048, NT=1024, TTW=512, R=512, H=16, HID=8192):
        self.G = G
        self.NG = NG
        self.NC = G * NG
        self.D = D
        self.DC = D // 128
        self.NT = NT
        self.TTW = TTW
        self.NTT = NT // TTW
        self.R = R
        self.RC = R // 128
        self.H = H
        self.DH = 128
        self.HID = HID
        self.HC = HID // 128
        self.BND = TTW // 128
        self.S = G * NT
        self.NTB = NT // 128
        self.SBLK = self.S // 128
        self.MW = 128 * (self.BND - 1) + TTW


FULL = CFG()

ENGINES = ("tensor", "vector", "scalar", "sync", "gpsimd")


class B:

    def __init__(self, nc, es):
        self.nc = nc
        self.es = es
        self.sems = {}
        self.cnt = {}
        self.state = {}
        self.role = None
        self.eng = None

    def mksem(self, *names):
        for n in names:
            self.sems[n] = self.es.enter_context(self.nc.semaphore(n))
            self.cnt[n] = 0

    def v(self, name):
        return self.cnt[name]

    def op(self, engine, emit, waits=(), inc=None, drain=True):
        if self.role == engine:
            if drain and engine in ("vector", "scalar"):
                self.eng.drain()
            for (s, val) in waits:
                if val > 0:
                    self.eng.wait_ge(self.sems[s], val)
            ins = emit(self.eng)
            if inc is not None and ins is not None:
                ins.then_inc(self.sems[inc[0]], inc[1])
        if inc is not None:
            self.cnt[inc[0]] += inc[1]

    def phase(self, name, fn):
        fns = fn if isinstance(fn, (list, tuple)) else [fn]
        import copy
        snap = dict(self.cnt)
        snap_state = copy.deepcopy(self.state)
        ends = []
        with self.nc.Block(name) as blk:
            for role in ENGINES:
                self.cnt = dict(snap)
                self.state = copy.deepcopy(snap_state)
                self.role = role

                def runner(eng, fns=fns):
                    self.eng = eng
                    for f in fns:
                        f()

                getattr(blk, role)(runner)
                ends.append((self.cnt, self.state))
        for e in ends[1:]:
            assert e == ends[0], f"phase {name}: inconsistent sem counters/state"
        self.cnt, self.state = ends[0]
        self.role = None


def build_program(cfg: CFG):
    nc = bass.Bass("TRN2", target_bir_lowering=False)
    c = cfg
    NT, TTW, NTT, DC, RC, H, HC, G = c.NT, c.TTW, c.NTT, c.DC, c.RC, c.H, c.HC, c.G

    d_x = nc.dram_tensor("x", [128, DC, NT], F32, kind="ExternalInput")
    d_wg = nc.dram_tensor("wg", [HC, 128, 2 * DC * 128], F32, kind="ExternalInput")
    d_wlin = nc.dram_tensor("wlin", [DC, 128, HC * 128], F32, kind="ExternalInput")
    d_wqkv = nc.dram_tensor("wqkv", [128, 3 * DC * RC * 128], F32, kind="ExternalInput")
    d_wqu = nc.dram_tensor("wqu", [H, 128, c.R], F32, kind="ExternalInput")
    d_wku = nc.dram_tensor("wku", [H, 128, c.R], F32, kind="ExternalInput")
    d_wvu = nc.dram_tensor("wvu", [H, 128, RC * 128], F32, kind="ExternalInput")
    d_wout = nc.dram_tensor("wout", [DC, 128, H * 128], F32, kind="ExternalInput")
    d_outb = nc.dram_tensor("outb", [128, DC], F32, kind="ExternalInput")
    d_mask = nc.dram_tensor("maskbig", [128, G, c.MW], BF16, kind="ExternalInput")
    d_cvec = nc.dram_tensor("cvec", [128, 8], F32, kind="ExternalInput")
    d_y = nc.dram_tensor("y", [128, DC, NT], F32, kind="ExternalOutput")
    d_wg_e4 = nc.dram_tensor("wg_e4", [HC, 128, 2 * DC * 128], E4)
    d_wlin_e4 = nc.dram_tensor("wlin_e4", [DC, 128, HC * 128], E4)
    d_xspill = nc.dram_tensor("xspill", [128, DC, NT], F32)
    d_ccin = nc.dram_tensor("cc_in", [2, RC, 128, NT], BF16)
    d_ccout = nc.dram_tensor("cc_out", [G, 2, RC, 128, NT], BF16)
    d_ccin_k = d_ccin.ap()[0]
    d_ccin_v = d_ccin.ap()[1]

    es_root = ExitStack()
    b = B(nc, es_root)
    b.mksem("sLX", "sCIN", "sSP", "sY", "sCW0", "sCW1", "sCC", "sKV",
            "sWG0", "sWG1", "sWL0", "sWL1", "sWU0", "sWU1", "sWO0", "sWO1",
            "sWG0b", "sWG1b", "sWL0b", "sWL1b", "sWA0", "sWA1", "sCIK", "sKV2", "sKV8",
            "sPE", "sVE", "sSE")
    S = b.sems

    with es_root:
        sb = lambda n, sh, dt: es_root.enter_context(nc.sbuf_tensor(n, sh, dt))
        t_cvec = sb("cvec_sb", [128, 8], F32)
        t_outb = sb("outb_sb", [128, DC], F32)
        t_ones_f = sb("ones_f", [128, 1], F32)
        t_ones_b = sb("ones_b", [128, 1], BF16)
        t_ones_r = sb("ones_r", [1, 128], F32)
        t_stats = sb("stats_sb", [1, 6 * TTW], F32)
        t_scrF = sb("scrF", [128, TTW], F32)
        t_sq = sb("sqscr", [128, 2 * TTW], F32)
        t_tA = sb("tA", [128, NT], BF16)
        t_tB = sb("tB", [128, NT], BF16)
        t_gbf = sb("g_bf", [128, NT], BF16)
        t_f1bf = sb("f1_bf", [128, NT], BF16)
        t_ffnbf = sb("ffn_bf", [128, 2 * TTW], BF16)

        PB = [es_root.enter_context(nc.psum_tensor(f"pb{i}", [128, 512], F32))
              for i in range(8)]
        b.state["bank"] = {i: None for i in range(8)}
        b.state["cw"] = {0: 0, 1: 0}

        class _BG:
            def __getitem__(self, i):
                return b.state["bank"][i]

            def __setitem__(self, i, v):
                b.state["bank"][i] = list(v) if v is not None else None

        bank_guard = _BG()

        class _CW:
            def __getitem__(self, i):
                return b.state["cw"][i]

            def __setitem__(self, i, v):
                b.state["cw"][i] = v

        cw_count_proxy = _CW()

        def guard_waits(banks):
            w = []
            for i in banks:
                g = b.state["bank"][i]
                if g is not None:
                    w.append(tuple(g))
            return w

        def p_preload(t_X):
            b.op("sync", lambda e: e.dma_start(out=t_cvec[:, :], in_=d_cvec.ap()), inc=("sLX", 16))
            b.op("sync", lambda e: e.dma_start(out=t_outb[:, :], in_=d_outb.ap()), inc=("sLX", 16))
            hDC = DC // 2
            b.op("sync", lambda e: e.dma_start(
                out=t_X[:, 0:hDC * NT].rearrange("p (c n) -> p c n", c=hDC),
                in_=d_x.ap()[:, 0:hDC]),
                inc=("sLX", 16))
            b.op("scalar", lambda e: e.dma_start(
                out=t_X[:, hDC * NT:DC * NT].rearrange("p (c n) -> p c n", c=hDC),
                in_=d_x.ap()[:, hDC:DC]),
                inc=("sLX", 16), drain=False)
            b.op("vector", lambda e: e.memset(t_ones_f[:, :], 1.0), inc=("sVE", 1))
            b.op("vector", lambda e: e.memset(t_ones_b[:, :], 1.0), inc=("sVE", 1))
            b.op("vector", lambda e: e.memset(t_ones_r[:, :], 1.0), inc=("sVE", 1))

        def p_ln(t_X, x_ready_waits, t_out):
            apply_done = []
            for T in range(NTT):
                bk = 4 * (T % 2)
                pS1, pS2, pMB, pIB = PB[bk], PB[bk + 1], PB[bk + 2], PB[bk + 3]

                def st(k):
                    return t_stats[:, k * TTW:(k + 1) * TTW]

                s2done = []
                for dc in range(DC):
                    xs = t_X[:, dc * NT + T * TTW: dc * NT + T * TTW + TTW]
                    sqs = t_sq[:, (dc % 2) * TTW:(dc % 2) * TTW + TTW]
                    wsq = list(x_ready_waits) if dc < 2 else [("sPE", s2done[dc - 2])]
                    b.op("scalar", lambda e, xs=xs, sqs=sqs: e.square(sqs, xs),
                         waits=wsq, inc=("sSE", 1), drain=False)
                    b.op("tensor", lambda e, xs=xs, pS1=pS1, dc=dc: e.matmul(
                        pS1[0:1, 0:TTW], t_ones_f[:, :], xs,
                        start=(dc == 0), stop=(dc == DC - 1)),
                        waits=(list(x_ready_waits) + guard_waits([bk])) if dc == 0 else (),
                        inc=("sPE", 1) if dc == DC - 1 else None)
                    b.op("tensor", lambda e, sqs=sqs, pS2=pS2, dc=dc: e.matmul(
                        pS2[0:1, 0:TTW], t_ones_f[:, :], sqs,
                        start=(dc == 0), stop=(dc == DC - 1)),
                        waits=[("sSE", b.v("sSE"))] + (guard_waits([bk + 1]) if dc == 0 else []),
                        inc=("sPE", 1))
                    s2done.append(b.v("sPE"))
                b.op("vector", lambda e, pS1=pS1: e.tensor_scalar(
                    st(0), pS1[0:1, 0:TTW], 1.0 / c.D, None, ALU.mult),
                    waits=[("sPE", b.v("sPE"))], inc=("sVE", 1))
                b.op("vector", lambda e, pS2=pS2: e.tensor_scalar(
                    st(1), pS2[0:1, 0:TTW], 1.0 / c.D, None, ALU.mult), inc=("sVE", 1))
                bank_guard[bk] = ("sVE", b.v("sVE"))
                bank_guard[bk + 1] = ("sVE", b.v("sVE"))
                b.op("vector", lambda e: e.tensor_tensor(st(2), st(0), st(0), ALU.mult), inc=("sVE", 1))
                b.op("vector", lambda e: e.tensor_tensor(st(2), st(1), st(2), ALU.subtract), inc=("sVE", 1))
                b.op("vector", lambda e: e.tensor_scalar(st(2), st(2), EPS, None, ALU.add), inc=("sVE", 1))
                b.op("scalar", lambda e: e.activation(st(3), st(2), AF.Sqrt),
                     waits=[("sVE", b.v("sVE"))], inc=("sSE", 1), drain=False)
                b.op("vector", lambda e: e.reciprocal(st(4), st(3)),
                     waits=[("sSE", b.v("sSE"))], inc=("sVE", 1))
                for _ in range(2):
                    b.op("vector", lambda e: e.tensor_tensor(st(5), st(4), st(4), ALU.mult), inc=("sVE", 1))
                    b.op("vector", lambda e: e.tensor_tensor(st(5), st(5), st(2), ALU.mult), inc=("sVE", 1))
                    b.op("vector", lambda e: e.tensor_scalar(st(5), st(5), -0.5, 1.5, ALU.mult, ALU.add), inc=("sVE", 1))
                    b.op("vector", lambda e: e.tensor_tensor(st(4), st(4), st(5), ALU.mult), inc=("sVE", 1))
                b.op("tensor", lambda e, pMB=pMB: e.matmul(
                    pMB[:, 0:TTW], t_ones_r[0:1, :], st(0), start=True, stop=True),
                    waits=[("sVE", b.v("sVE"))] + guard_waits([bk + 2]), inc=("sPE", 1))
                b.op("tensor", lambda e, pIB=pIB: e.matmul(
                    pIB[:, 0:TTW], t_ones_r[0:1, :], st(4), start=True, stop=True),
                    waits=guard_waits([bk + 3]), inc=("sPE", 1))
                for dc in range(DC):
                    xs = t_X[:, dc * NT + T * TTW: dc * NT + T * TTW + TTW]
                    outs = t_out[:, dc * NT + T * TTW: dc * NT + T * TTW + TTW]
                    b.op("vector", lambda e, xs=xs, pMB=pMB: e.tensor_tensor(
                        t_scrF[:, :], xs, pMB[:, 0:TTW], ALU.subtract),
                        waits=[("sPE", b.v("sPE"))] if dc == 0 else (), inc=("sVE", 1))
                    b.op("vector", lambda e, outs=outs, pIB=pIB: e.tensor_tensor(
                        outs, t_scrF[:, :], pIB[:, 0:TTW], ALU.mult), inc=("sVE", 1))
                bank_guard[bk + 2] = ("sVE", b.v("sVE"))
                bank_guard[bk + 3] = ("sVE", b.v("sVE"))
                apply_done.append(b.v("sVE"))
            return apply_done

        cw_count = cw_count_proxy

        def p_ffn(t_X, t_xhat, xhat_ready, t_act, wgbuf, wlbuf, cache_write,
                  space_ready=()):
            sfx = "" if cache_write else "b"
            fill_val = {}
            pe_done_h = {}
            ve_done_h = {}
            prev_cast_se = {"v": 0}

            def fill_wg(hf, wts):
                bufp = hf % 2
                if cache_write:
                    if cw_count[bufp] > 0:
                        wts = wts + [(f"sCW{bufp}", 16 * cw_count[bufp])]
                    b.op("gpsimd", lambda e, hf=hf, bufp=bufp: e.dma_start(
                        out=wgbuf[bufp][:, :], in_=d_wg.ap()[hf]),
                        waits=wts, inc=(f"sWG{bufp}{sfx}", 16))
                else:
                    b.op("sync", lambda e, hf=hf, bufp=bufp: e.dma_start(
                        out=wgbuf[bufp][:, :], in_=d_wg_e4.ap()[hf]),
                        waits=wts, inc=(f"sWG{bufp}{sfx}", 16))
                fill_val[hf] = (f"sWG{bufp}{sfx}", b.v(f"sWG{bufp}{sfx}"))
                if cache_write:
                    b.op("sync", lambda e, hf=hf, bufp=bufp: e.dma_start(
                        out=d_wg_e4.ap()[hf], in_=wgbuf[bufp][:, :]),
                        waits=[fill_val[hf]],
                        inc=(f"sCW{bufp}", 16))
                    cw_count[bufp] += 1

            for hf in range(min(2, HC)):
                fill_wg(hf, list(space_ready))
            for h in range(HC):
                bufp = h % 2
                bb = 4 * (h % 2)
                first_w = [fill_val[h]]
                if h >= 2:
                    first_w.append(("sVE", ve_done_h[h - 2]))
                first_w += guard_waits([bb + i for i in range(2 * NTT)])
                first = True
                firstT = [True] * NTT
                for gate in range(2):
                    for j in range(DC // 2):
                        base = (gate * DC + 2 * j) * 128
                        lh = wgbuf[bufp][:, base:base + 256].rearrange("p (i m) -> p i m", i=2)
                        xsl = t_xhat[:, 2 * j * NT:(2 * j + 2) * NT].rearrange(
                            "p (i n) -> p i n", i=2)
                        for T in range(NTT):
                            pb = PB[bb + gate * NTT + T]
                            rhs = xsl[:, :, T * TTW:T * TTW + TTW]
                            stop = (j == DC // 2 - 1)
                            w_ = list(first_w) if first else []
                            if firstT[T]:
                                w_.append(("sVE", xhat_ready[T]))
                                firstT[T] = False
                            b.op("tensor", lambda e, pb=pb, lh=lh, rhs=rhs, j=j, stop=stop: e.matmul(
                                pb[:, 0:TTW], lh, rhs, start=(j == 0), stop=stop,
                                perf_mode=DRMODE),
                                waits=w_,
                                inc=("sPE", 1) if stop else None)
                            first = False
                pe_done_h[h] = b.v("sPE")
                if h + 2 < HC:
                    fill_wg(h + 2, [("sPE", pe_done_h[h])])
                wv = [("sPE", b.v("sPE"))]
                for T in range(NTT):
                    pbg = PB[bb + T]
                    pbf = PB[bb + NTT + T]
                    b.op("vector", lambda e, T=T, pbg=pbg: e.tensor_copy(
                        t_gbf[:, T * TTW:T * TTW + TTW], pbg[:, 0:TTW]),
                        waits=wv if T == 0 else (), inc=("sVE", 1))
                    b.op("scalar", lambda e, T=T, pbf=pbf: e.activation(
                        t_f1bf[:, T * TTW:T * TTW + TTW], pbf[:, 0:TTW], AF.Identity),
                        waits=(wv + [("sVE", b.v("sVE"))]) if T == 0 else (),
                        inc=("sSE", 1), drain=False)
                ve_done_h[h] = b.v("sVE")
                se_copy_h = b.v("sSE")
                for i in range(2 * NTT):
                    bank_guard[bb + i] = ("sVE", ve_done_h[h]) if i < NTT else ("sSE", se_copy_h)
                b.op("scalar", lambda e: e.square(t_tA[:, :], t_gbf[:, :]),
                     waits=[("sVE", b.v("sVE")), ("sSE", prev_cast_se["v"])],
                     inc=("sSE", 1), drain=False)
                b.op("vector", lambda e: e.tensor_tensor(t_tB[:, :], t_tA[:, :], t_gbf[:, :], ALU.mult),
                     waits=[("sSE", b.v("sSE"))], inc=("sVE", 1), drain=False)
                b.op("vector", lambda e: e.tensor_scalar(
                    t_tA[:, :], t_tB[:, :], C1, None, ALU.mult), inc=("sVE", 1))
                b.op("vector", lambda e: e.tensor_tensor(
                    t_tB[:, :], t_gbf[:, :], t_tA[:, :], ALU.add), inc=("sVE", 1))
                b.op("vector", lambda e: e.tensor_scalar(
                    t_tA[:, :], t_tB[:, :], C2, None, ALU.mult), inc=("sVE", 1))
                b.op("scalar", lambda e: e.activation(t_tB[:, :], t_tA[:, :], AF.Tanh),
                     waits=[("sVE", b.v("sVE"))], inc=("sSE", 1), drain=False)
                b.op("vector", lambda e: e.tensor_scalar(
                    t_tA[:, :], t_tB[:, :], 1.0, None, ALU.add),
                    waits=[("sSE", b.v("sSE"))], inc=("sVE", 1), drain=False)
                b.op("vector", lambda e: e.scalar_tensor_tensor(
                    t_tB[:, :], t_tA[:, :], 0.5, t_gbf[:, :], ALU.mult, ALU.mult), inc=("sVE", 1))
                b.op("vector", lambda e: e.tensor_tensor(
                    t_tA[:, :], t_tB[:, :], t_f1bf[:, :], ALU.mult),
                    waits=[("sSE", se_copy_h)], inc=("sVE", 1))
                b.op("scalar", lambda e, h=h: e.copy(
                    t_act[:, h * NT:(h + 1) * NT], t_tA[:, :]),
                    waits=[("sVE", b.v("sVE"))], inc=("sSE", 1), drain=False)
                prev_cast_se["v"] = b.v("sSE")
            act_ready = ("sSE", b.v("sSE"))

            fill3_val = {}
            pe_done_oc = {}
            ve_done_oc = {}

            def fill_wl(of, wts):
                bufp = of % 2
                if cache_write:
                    if cw_count[bufp] > 0:
                        wts = wts + [(f"sCW{bufp}", 16 * cw_count[bufp])]
                    b.op("gpsimd", lambda e, of=of, bufp=bufp: e.dma_start(
                        out=wlbuf[bufp][:, :], in_=d_wlin.ap()[of]),
                        waits=wts, inc=(f"sWL{bufp}{sfx}", 16))
                else:
                    b.op("sync", lambda e, of=of, bufp=bufp: e.dma_start(
                        out=wlbuf[bufp][:, :], in_=d_wlin_e4.ap()[of]),
                        waits=wts, inc=(f"sWL{bufp}{sfx}", 16))
                fill3_val[of] = (f"sWL{bufp}{sfx}", b.v(f"sWL{bufp}{sfx}"))
                if cache_write:
                    b.op("sync", lambda e, of=of, bufp=bufp: e.dma_start(
                        out=d_wlin_e4.ap()[of], in_=wlbuf[bufp][:, :]),
                        waits=[fill3_val[of]],
                        inc=(f"sCW{bufp}", 16))
                    cw_count[bufp] += 1

            for of in range(min(2, DC)):
                gate_done = [("sPE", pe_done_h[HC - 2 + (of % 2)])]
                fill_wl(of, list(space_ready) + gate_done)
            for oc in range(DC):
                bufp = oc % 2
                banks = [2 * (oc % 2) + T for T in range(NTT)]
                first_w = [fill3_val[oc], act_ready]
                if oc >= 2:
                    first_w.append(("sVE", ve_done_oc[oc - 2]))
                first_w += guard_waits(banks)
                first = True
                for jc in range(HC // 2):
                    lh = wlbuf[bufp][:, 2 * jc * 128:2 * jc * 128 + 256].rearrange(
                        "p (i m) -> p i m", i=2)
                    asl = t_act[:, 2 * jc * NT:(2 * jc + 2) * NT].rearrange(
                        "p (i n) -> p i n", i=2)
                    for T in range(NTT):
                        pb = PB[banks[T]]
                        rhs = asl[:, :, T * TTW:T * TTW + TTW]
                        stop = (jc == HC // 2 - 1)
                        b.op("tensor", lambda e, pb=pb, lh=lh, rhs=rhs, jc=jc, stop=stop: e.matmul(
                            pb[:, 0:TTW], lh, rhs, start=(jc == 0), stop=stop,
                            perf_mode=DRMODE),
                            waits=first_w if first else (),
                            inc=("sPE", 1) if stop else None)
                        first = False
                pe_done_oc[oc] = b.v("sPE")
                if oc + 2 < DC:
                    fill_wl(oc + 2, [("sPE", pe_done_oc[oc])])
                wv = [("sPE", b.v("sPE"))]
                for T in range(NTT):
                    pb = PB[banks[T]]
                    fb = t_ffnbf[:, T * TTW:T * TTW + TTW]
                    xs = t_X[:, oc * NT + T * TTW: oc * NT + T * TTW + TTW]
                    b.op("vector", lambda e, pb=pb, fb=fb: e.tensor_copy(fb, pb[:, 0:TTW]),
                         waits=wv if T == 0 else (), inc=("sVE", 1))
                    b.op("vector", lambda e, fb=fb, xs=xs: e.scalar_tensor_tensor(
                        xs, fb, 0.5, xs, ALU.mult, ALU.add), inc=("sVE", 1))
                ve_done_oc[oc] = b.v("sVE")
                for i in banks:
                    bank_guard[i] = ("sVE", ve_done_oc[oc])
            return ve_done_oc

        esX = ExitStack()
        t_X = esX.enter_context(nc.sbuf_tensor("X1", [128, DC * NT], F32, side="right"))

        es1 = ExitStack()
        t_xhat_e4 = es1.enter_context(nc.sbuf_tensor("xhat_e4", [128, DC * NT], E4))
        t_act = es1.enter_context(nc.sbuf_tensor("act_e4", [128, HC * NT], E4))
        wbufs = [es1.enter_context(nc.sbuf_tensor(f"wbuf{i}", [128, max(2 * DC, HC) * 128], E4)) for i in range(2)]
        wgbuf = [t[:, 0:2 * DC * 128] for t in wbufs]
        wlbuf = [t[:, 0:HC * 128] for t in wbufs]

        ln1_ve = [None]
        b.phase("blk1", [
            lambda: p_preload(t_X),
            lambda: ln1_ve.__setitem__(0, p_ln(
                t_X, [("sLX", b.v("sLX")), ("sVE", b.v("sVE"))], t_xhat_e4)),
            lambda: p_ffn(t_X, t_xhat_e4, ln1_ve[0], t_act, wgbuf, wlbuf, True),
        ])
        

        es1.close()

        esQ = ExitStack()
        t_Qd = esQ.enter_context(nc.sbuf_tensor("Qd", [128, RC * NT], E4))
        t_wqu_b = [esQ.enter_context(nc.sbuf_tensor(f"wqu_b{i}", [128, c.R], BF16)) for i in range(2)]
        t_wku_b = [esQ.enter_context(nc.sbuf_tensor(f"wku_b{i}", [128, c.R], BF16)) for i in range(2)]
        t_wvu2 = esQ.enter_context(nc.sbuf_tensor("wvu2", [128, RC * 2 * 128], BF16))
        es2 = ExitStack()
        t_xhat_bf = es2.enter_context(nc.sbuf_tensor("xhat_bf", [128, DC * NT], BF16))
        t_wqkv = [es2.enter_context(nc.sbuf_tensor(f"wqkv_sb{i}", [128, DC * RC * 128], BF16))
                  for i in range(2)]
        t_Kd = es2.enter_context(nc.sbuf_tensor("Kd_loc", [128, RC * NT], BF16))
        t_Vd = es2.enter_context(nc.sbuf_tensor("Vd_loc", [128, RC * NT], BF16))

        resid1 = b.v("sVE")
        cw_final = [("sCW0", b.v("sCW0")), ("sCW1", b.v("sCW1"))]
        ln2_ve = [None]

        wu_val0 = {}

        def phase_qkv():
            pe_proj_done = {}
            wfill = {}

            def fill_wqkv(p, wts):
                par = p % 2
                view = d_wqkv.ap()[:, p * DC * RC * 128:(p + 1) * DC * RC * 128]
                b.op("gpsimd", lambda e, par=par, view=view: e.dma_start(
                    out=t_wqkv[par][:, :], in_=view), waits=wts, inc=(f"sWA{par}", 16))
                wfill[p] = (f"sWA{par}", b.v(f"sWA{par}"))

            fill_wqkv(0, cw_final)
            fill_wqkv(1, [])
            hDC = DC // 2
            b.op("sync", lambda e: e.dma_start(
                out=d_xspill.ap()[:, 0:hDC],
                in_=t_X[:, 0:hDC * NT].rearrange("p (c n) -> p c n", c=hDC)),
                waits=[("sVE", ln2_ve[0][-1])], inc=("sSP", 16))
            b.op("scalar", lambda e: e.dma_start(
                out=d_xspill.ap()[:, hDC:DC],
                in_=t_X[:, hDC * NT:DC * NT].rearrange("p (c n) -> p c n", c=hDC)),
                waits=[("sVE", ln2_ve[0][-1])], inc=("sSP", 16), drain=False)
            outs = {0: t_Kd, 1: t_Vd, 2: t_Qd}
            kd_vd_done = [0]
            for proj in range(3):
                for rc in range(RC):
                    for T in range(NTT):
                        k = (proj * RC * NTT + rc * NTT + T) % 4
                        pb = PB[k]
                        first_w = [wfill[proj], ("sVE", ln2_ve[0][T])] + guard_waits([k])
                        for dc in range(DC):
                            lh = t_wqkv[proj % 2][:, (dc * RC + rc) * 128:
                                                  (dc * RC + rc) * 128 + 128]
                            rhs = t_xhat_bf[:, dc * NT + T * TTW: dc * NT + T * TTW + TTW]
                            stop = (dc == DC - 1)
                            b.op("tensor", lambda e, pb=pb, lh=lh, rhs=rhs, dc=dc, stop=stop: e.matmul(
                                pb[:, 0:TTW], lh, rhs, start=(dc == 0), stop=stop),
                                waits=first_w if dc == 0 else (),
                                inc=("sPE", 1) if stop else None)
                        dst = outs[proj][:, rc * NT + T * TTW: rc * NT + T * TTW + TTW]
                        b.op("vector", lambda e, dst=dst, pb=pb: e.tensor_copy(dst, pb[:, 0:TTW]),
                             waits=[("sPE", b.v("sPE"))], inc=("sVE", 1), drain=False)
                        bank_guard[k] = ("sVE", b.v("sVE"))
                pe_proj_done[proj] = b.v("sPE")
                if proj + 2 < 3:
                    fill_wqkv(proj + 2, [("sPE", pe_proj_done[proj])])
                if proj == 0:
                    b.op("sync", lambda e: e.dma_start(
                        out=d_ccin_k.transpose([1, 0, 2]),
                        in_=t_Kd[:, :].rearrange("p (rc t) -> p rc t", rc=RC)),
                        waits=[("sVE", b.v("sVE"))], inc=("sCIK", 16))
                if proj == 1:
                    kd_vd_done[0] = b.v("sVE")
            b.op("sync", lambda e: e.dma_start(
                out=d_ccin_v.transpose([1, 0, 2]),
                in_=t_Vd[:, :].rearrange("p (rc t) -> p rc t", rc=RC)),
                waits=[("sVE", kd_vd_done[0])], inc=("sCIN", 16))

        b.phase("blk2", [
            lambda: ln2_ve.__setitem__(
                0, p_ln(t_X, [("sVE", resid1)] + cw_final, t_xhat_bf)),
            phase_qkv,
        ])
        es2.close()
        esX.close()

        esC = ExitStack()
        t_ctx = esC.enter_context(nc.sbuf_tensor("ctx", [128, H * NT], BF16))
        es5 = ExitStack()
        t_mask = es5.enter_context(nc.sbuf_tensor("mask_sb", [128, G * c.MW], BF16))
        t_binv = es5.enter_context(nc.sbuf_tensor("binv", [128, TTW], F32))
        t_Kda = es5.enter_context(nc.sbuf_tensor("Kd_all", [128, RC * c.S], E4))
        t_Vda = es5.enter_context(nc.sbuf_tensor("Vd_all", [128, RC * c.S], BF16))
        t_Qm = [es5.enter_context(nc.sbuf_tensor(f"Qm{i}", [128, RC * NT], E4)) for i in range(2)]
        t_M = [es5.enter_context(nc.sbuf_tensor(f"M_sb{i}", [128, RC * c.R], E4)) for i in range(2)]
        t_Vup = es5.enter_context(nc.sbuf_tensor("Vup", [128, c.SBLK * 2 * 128], BF16))
        t_E = es5.enter_context(nc.sbuf_tensor("E_sb", [128, 4 * TTW], BF16))


        def phase_kvload():
            for hh in range(min(2, H)):
                par = hh % 2
                wts = [("sCIK", 16), ("sCIN", 16), ("sSP", 32)] if hh == 0 else []
                b.op("gpsimd", lambda e, hh=hh, par=par: e.dma_start(
                    out=t_wqu_b[par][:, :], in_=d_wqu.ap()[hh]), waits=wts, inc=(f"sWU{par}", 16))
                b.op("gpsimd", lambda e, hh=hh, par=par: e.dma_start(
                    out=t_wku_b[par][:, :], in_=d_wku.ap()[hh]), inc=(f"sWU{par}", 16))
                view = t_wvu2[:, :].rearrange("p (rc q n) -> p rc q n", q=2, n=128)[:, :, par]
                b.op("gpsimd", lambda e, hh=hh, view=view: e.dma_start(
                    out=view, in_=d_wvu.ap()[hh].rearrange("p (rc n) -> p rc n", n=128)),
                    inc=(f"sWU{par}", 16))
                wu_val0[hh] = (f"sWU{par}", b.v(f"sWU{par}"))
            rgs = [[g * G + i for i in range(G)] for g in range(c.NG)]
            b.op("gpsimd", lambda e: e.collective_compute(
                "AllGather", ALU.bypass, replica_groups=rgs,
                ins=[d_ccin.ap().opt()], outs=[d_ccout.ap().opt()]),
                waits=[("sCIK", 16), ("sCIN", 16)], inc=("sCC", 1))
            b.op("sync", lambda e: e.dma_start(
                out=t_mask[:, :].rearrange("p (g w) -> p g w", g=G), in_=d_mask.ap()),
                waits=[("sSP", 32), ("sCIK", 16)], inc=("sKV", 16))
            for kv, dst, sem, engq in ((0, t_Kda, "sKV8", "gpsimd"), (1, t_Vda, "sKV2", "scalar")):
                for gi in range(G):
                    view = dst[:, :].rearrange("p (rc g t) -> p rc g t", rc=RC, g=G)[:, :, gi]
                    b.op(engq, lambda e, view=view, gi=gi, kv=kv: e.dma_start(
                        out=view, in_=d_ccout.ap()[gi, kv].transpose([1, 0, 2])),
                        waits=[("sCC", 1)] if gi == 0 else (),
                        inc=(sem, 16), drain=False)


        def phase_heads():
            ctx_all = [0]
            wu_val = {}
            qm_ready = {}
            vup_ready = [0]
            pending_fin = []
            b.state["lg"] = 0

            def emit_loads(hh, wts):
                par = hh % 2
                b.op("gpsimd", lambda e, hh=hh, par=par: e.dma_start(
                    out=t_wqu_b[par][:, :], in_=d_wqu.ap()[hh]), waits=wts, inc=(f"sWU{par}", 16))
                b.op("gpsimd", lambda e, hh=hh, par=par: e.dma_start(
                    out=t_wku_b[par][:, :], in_=d_wku.ap()[hh]), inc=(f"sWU{par}", 16))
                view = t_wvu2[:, :].rearrange("p (rc q n) -> p rc q n", q=2, n=128)[:, :, par]
                b.op("gpsimd", lambda e, hh=hh, view=view: e.dma_start(
                    out=view, in_=d_wvu.ap()[hh].rearrange("p (rc n) -> p rc n", n=128)),
                    inc=(f"sWU{par}", 16))
                wu_val[hh] = (f"sWU{par}", b.v(f"sWU{par}"))

            def emit_mqm(hh):
                par = hh % 2
                for qc in range(RC):
                    pb = PB[qc % 2]
                    b.op("tensor", lambda e, pb=pb, qc=qc, par=par: e.matmul(
                        pb[:, 0:c.R], t_wqu_b[par][:, qc * 128:qc * 128 + 128],
                        t_wku_b[par][:, :], start=True, stop=True),
                        waits=([wu_val[hh]] if qc == 0 else []) + guard_waits([qc % 2]),
                        inc=("sPE", 1))
                    dst = t_M[par][:, qc * c.R:(qc + 1) * c.R]
                    b.op("vector", lambda e, dst=dst, pb=pb: e.tensor_scalar(
                        dst, pb[:, 0:c.R], RSQ_DH * 64.0, None, ALU.mult),
                        waits=[("sPE", b.v("sPE"))], inc=("sVE", 1), drain=False)
                    bank_guard[qc % 2] = ("sVE", b.v("sVE"))
                m_ready = b.v("sVE")
                for rk in range(RC):
                    for T in range(NTT):
                        k = (rk * NTT + T) % 4
                        pb = PB[k]
                        first_w = [("sVE", m_ready)] + guard_waits([k])
                        m3 = t_M[par][:, :].rearrange("p (qc r) -> p qc r", qc=RC)
                        qd3 = t_Qd[:, :].rearrange("p (qc t) -> p qc t", qc=RC)
                        for j in range(RC // 2):
                            lh = m3[:, 2 * j:2 * j + 2, rk * 128: rk * 128 + 128]
                            rhs = qd3[:, 2 * j:2 * j + 2, T * TTW: T * TTW + TTW]
                            stop = (j == RC // 2 - 1)
                            b.op("tensor", lambda e, pb=pb, lh=lh, rhs=rhs, j=j, stop=stop: e.matmul(
                                pb[:, 0:TTW], lh, rhs, start=(j == 0), stop=stop,
                                perf_mode=DRMODE),
                                waits=first_w if j == 0 else (),
                                inc=("sPE", 1) if stop else None)
                        dst = t_Qm[par][:, rk * NT + T * TTW: rk * NT + T * TTW + TTW]
                        b.op("vector", lambda e, dst=dst, pb=pb: e.tensor_scalar(
                            dst, pb[:, 0:TTW], 16.0, None, ALU.mult),
                             waits=[("sPE", b.v("sPE"))], inc=("sVE", 1), drain=False)
                        bank_guard[k] = ("sVE", b.v("sVE"))
                qm_ready[par] = b.v("sVE")

            wu_val.update(wu_val0)
            emit_mqm(0)
            if H > 1:
                emit_mqm(1)

            for h in range(H):
                par = h % 2
                if h % 2 == 0:
                    for sbk in range(c.SBLK):
                        pb = PB[2 + (sbk % 2)]
                        first_w = ([("sKV2", 16 * G)] if (h == 0 and sbk == 0) else []) + \
                            ([wu_val[h], wu_val.get(h + 1, wu_val[h])] if sbk == 0 else []) + \
                            guard_waits([2 + (sbk % 2)])
                        for rc in range(RC):
                            lh = t_Vda[:, rc * c.S + sbk * 128: rc * c.S + sbk * 128 + 128]
                            rhs = t_wvu2[:, rc * 256:rc * 256 + 256]
                            stop = (rc == RC - 1)
                            b.op("tensor", lambda e, pb=pb, lh=lh, rhs=rhs, rc=rc, stop=stop: e.matmul(
                                pb[:, 0:256], lh, rhs, start=(rc == 0), stop=stop),
                                waits=first_w if rc == 0 else (),
                                inc=("sPE", 1) if stop else None)
                        dst = t_Vup[:, sbk * 256:(sbk + 1) * 256]
                        b.op("vector", lambda e, dst=dst, pb=pb: e.tensor_copy(dst, pb[:, 0:256]),
                             waits=[("sPE", b.v("sPE"))], inc=("sVE", 1), drain=False)
                        bank_guard[2 + (sbk % 2)] = ("sVE", b.v("sVE"))
                    vup_ready[0] = b.v("sVE")
                    if h + 2 < H:
                        emit_loads(h + 2, [("sPE", b.v("sPE"))])
                    if h + 3 < H:
                        emit_loads(h + 3, [("sPE", b.v("sPE"))])
                for T in range(NTT):
                    pbC = PB[4 + (T % 2)]
                    pbL = PB[6 + (T % 2)]
                    blocks = [(r, m) for m in range(c.BND * (T + 1)) for r in range(G)]
                    n_bl = len(blocks)
                    exp_done = {}
                    edep = {}
                    ctxsum_done = {}

                    lgslot = {}

                    def emit_logits(i):
                        r, m = blocks[i]
                        lgslot[i] = b.state["lg"] % 4
                        b.state["lg"] += 1
                        pbLg = PB[lgslot[i]]
                        first_w = (([("sVE", qm_ready[par]), ("sVE", vup_ready[0])] +
                                    ([("sKV", 16), ("sKV8", 16 * G)] if (h == 0 and T == 0) else []))
                                   if i == 0 else []) + guard_waits([lgslot[i]])
                        kda3 = t_Kda[:, :].rearrange("p (rc s) -> p rc s", rc=RC)
                        qm3 = t_Qm[par][:, :].rearrange("p (rc t) -> p rc t", rc=RC)
                        for j in range(RC // 2):
                            lh = kda3[:, 2 * j:2 * j + 2,
                                      r * NT + m * 128: r * NT + m * 128 + 128]
                            rhs = qm3[:, 2 * j:2 * j + 2, T * TTW: T * TTW + TTW]
                            stop = (j == RC // 2 - 1)
                            b.op("tensor", lambda e, pbLg=pbLg, lh=lh, rhs=rhs, j=j, stop=stop: e.matmul(
                                pbLg[:, 0:TTW], lh, rhs, start=(j == 0), stop=stop,
                                perf_mode=DRMODE),
                                waits=first_w if j == 0 else (),
                                inc=("sPE", 1) if stop else None)
                        ew = [("sPE", b.v("sPE"))]
                        if i >= 4:
                            ew.append(("sPE", ctxsum_done[i - 4]))
                        esl = t_E[:, (i % 4) * TTW:(i % 4) * TTW + TTW]
                        b.op("scalar", lambda e, esl=esl, pbLg=pbLg: e.activation(
                            esl, pbLg[:, 0:TTW], AF.Exp, bias=t_cvec[:, 1:2],
                            scale=t_cvec[:, 3:4]),
                            waits=ew, inc=("sSE", 1), drain=False)
                        exp_done[i] = b.v("sSE")
                        bank_guard[lgslot[i]] = ("sSE", b.v("sSE"))
                        masked = (m >= c.BND * T)
                        dep = [("sSE", exp_done[i])]
                        if masked:
                            dlt = m - c.BND * T
                            msl = t_mask[:, r * c.MW + 128 * (c.BND - 1 - dlt):
                                         r * c.MW + 128 * (c.BND - 1 - dlt) + TTW]
                            b.op("vector", lambda e, esl=esl, msl=msl: e.tensor_tensor(
                                esl, esl, msl, ALU.mult),
                                waits=[("sSE", b.v("sSE"))], inc=("sVE", 1), drain=False)
                            dep.append(("sVE", b.v("sVE")))
                        edep[i] = (esl, dep)

                    def emit_ctxsum(i):
                        r, m = blocks[i]
                        esl, dep = edep[i]
                        sbid = r * c.NTB + m
                        cw = dep[-1:]
                        if i == 0:
                            cw = list(dep) + guard_waits([4 + (T % 2), 6 + (T % 2)])
                        lhv = t_Vup[:, sbid * 256 + par * 128: sbid * 256 + par * 128 + 128]
                        b.op("tensor", lambda e, pbC=pbC, lhv=lhv, esl=esl, i=i: e.matmul(
                            pbC[:, 0:TTW], lhv, esl, start=(i == 0), stop=(i == n_bl - 1)),
                            waits=cw)
                        b.op("tensor", lambda e, esl=esl, i=i: e.matmul(
                            pbL[0:1, 0:TTW], t_ones_b[:, :], esl,
                            start=(i == 0), stop=(i == n_bl - 1)),
                            inc=("sPE", 1))
                        ctxsum_done[i] = b.v("sPE")

                    for i in range(min(3, n_bl)):
                        emit_logits(i)
                    if pending_fin:
                        pending_fin.pop(0)()
                    for i in range(n_bl):
                        if i + 3 < n_bl:
                            emit_logits(i + 3)
                        emit_ctxsum(i)

                    def finalize(h=h, T=T, pbC=pbC, pbL=pbL,
                                 pe_after=b.v("sPE")):
                        def stl(k):
                            return t_stats[:, k * TTW:(k + 1) * TTW]
                        b.op("vector", lambda e: e.reciprocal(stl(0), pbL[0:1, 0:TTW]),
                             waits=[("sPE", pe_after)], inc=("sVE", 1), drain=False)
                        bank_guard[6 + (T % 2)] = ("sVE", b.v("sVE"))
                        slot = b.state["lg"] % 4
                        b.state["lg"] += 1
                        b.op("tensor", lambda e, slot=slot: e.matmul(
                            PB[slot][:, 0:TTW], t_ones_r[0:1, :], stl(0), start=True, stop=True),
                            waits=[("sVE", b.v("sVE"))] + guard_waits([slot]), inc=("sPE", 1))
                        b.op("vector", lambda e, slot=slot: e.tensor_copy(t_binv[:, :], PB[slot][:, 0:TTW]),
                             waits=[("sPE", b.v("sPE"))], inc=("sVE", 1))
                        bank_guard[slot] = ("sVE", b.v("sVE"))
                        dst = t_ctx[:, h * NT + T * TTW: h * NT + T * TTW + TTW]
                        b.op("vector", lambda e, dst=dst: e.tensor_tensor(
                            dst, pbC[:, 0:TTW], t_binv[:, :], ALU.mult), inc=("sVE", 1))
                        bank_guard[4 + (T % 2)] = ("sVE", b.v("sVE"))

                    pending_fin.append(finalize)
                if h + 2 < H:
                    emit_mqm(h + 2)
            while pending_fin:
                pending_fin.pop(0)()
            ctx_all[0] = b.v("sVE")
            return ctx_all[0]

        ctx_ready = [None]
        b.phase("blk3", [phase_kvload,
                         lambda: ctx_ready.__setitem__(0, phase_heads())])
        es5.close()

        es6 = ExitStack()
        t_X2 = es6.enter_context(nc.sbuf_tensor("X2", [128, DC * NT], F32, side="right"))
        esA = ExitStack()
        t_attnscr = esA.enter_context(nc.sbuf_tensor("attnscr", [128, 2 * TTW], F32))
        t_wo_b = [esA.enter_context(nc.sbuf_tensor(f"wo_b{i}", [128, H * 128], BF16)) for i in range(2)]

        def phase_outproj():
            hDC = DC // 2
            b.op("sync", lambda e: e.dma_start(
                out=t_X2[:, 0:hDC * NT].rearrange("p (c n) -> p c n", c=hDC),
                in_=d_xspill.ap()[:, 0:hDC]),
                waits=[("sSP", 32)], inc=("sSP", 16))
            b.op("scalar", lambda e: e.dma_start(
                out=t_X2[:, hDC * NT:DC * NT].rearrange("p (c n) -> p c n", c=hDC),
                in_=d_xspill.ap()[:, hDC:DC]),
                waits=[("sSP", 32)], inc=("sSP", 16), drain=False)
            fillo_val = {}
            pe_done = {}
            ve_done = {}
            se_done = {}

            def fill_wo(of, wts):
                par = of % 2
                b.op("gpsimd", lambda e, of=of, par=par: e.dma_start(
                    out=t_wo_b[par][:, :], in_=d_wout.ap()[of]),
                    waits=wts, inc=(f"sWO{par}", 16))
                fillo_val[of] = (f"sWO{par}", b.v(f"sWO{par}"))

            for of in range(min(2, DC)):
                fill_wo(of, [])
            for oc in range(DC):
                par = oc % 2
                banks = [2 * (oc % 2) + T for T in range(NTT)]
                first_w = [fillo_val[oc], ("sVE", ctx_ready[0])] + guard_waits(banks)
                first = True
                for hh in range(H):
                    lh = t_wo_b[par][:, hh * 128:hh * 128 + 128]
                    for T in range(NTT):
                        pb = PB[banks[T]]
                        rhs = t_ctx[:, hh * NT + T * TTW: hh * NT + T * TTW + TTW]
                        stop = (hh == H - 1)
                        b.op("tensor", lambda e, pb=pb, lh=lh, rhs=rhs, hh=hh, stop=stop: e.matmul(
                            pb[:, 0:TTW], lh, rhs, start=(hh == 0), stop=stop),
                            waits=first_w if first else (),
                            inc=("sPE", 1) if stop else None)
                        first = False
                pe_done[oc] = b.v("sPE")
                if oc + 2 < DC:
                    fill_wo(oc + 2, [("sPE", pe_done[oc])])
                for T in range(NTT):
                    pb = PB[banks[T]]
                    asl = t_attnscr[:, T * TTW:T * TTW + TTW]
                    xs = t_X2[:, oc * NT + T * TTW: oc * NT + T * TTW + TTW]
                    sw = [("sPE", pe_done[oc])]
                    if oc >= 1:
                        sw.append(("sVE", ve_done[oc - 1]))
                    b.op("scalar", lambda e, asl=asl, pb=pb, oc=oc: e.activation(
                        asl, pb[:, 0:TTW], AF.Identity, bias=t_outb[:, oc:oc + 1]),
                        waits=sw, inc=("sSE", 1), drain=False)
                    vw = [("sSE", b.v("sSE"))]
                    if oc == 0 and T == 0:
                        vw.append(("sSP", 64))
                    b.op("vector", lambda e, asl=asl, xs=xs: e.tensor_tensor(
                        xs, xs, asl, ALU.add), waits=vw, inc=("sVE", 1))
                ve_done[oc] = b.v("sVE")
                se_done[oc] = b.v("sSE")
                for i in banks:
                    bank_guard[i] = ("sSE", se_done[oc])

        b.phase("outproj", phase_outproj)
        esA.close()
        esC.close()

        es7 = ExitStack()
        t_xhat_e4b = es7.enter_context(nc.sbuf_tensor("xhat_e4b", [128, DC * NT], E4))
        t_act2 = es7.enter_context(nc.sbuf_tensor("act_e4b", [128, HC * NT], E4))
        wbufs2 = [es7.enter_context(nc.sbuf_tensor(f"wbuf2_{i}", [128, max(2 * DC, HC) * 128], E4)) for i in range(2)]
        wgbuf2 = [t[:, 0:2 * DC * 128] for t in wbufs2]
        wlbuf2 = [t[:, 0:HC * 128] for t in wbufs2]

        ffn2_oc = [None]

        def phase_store():
            for oc in range(DC):
                b.op("sync", lambda e, oc=oc: e.dma_start(
                    out=d_y.ap()[:, oc],
                    in_=t_X2[:, oc * NT:(oc + 1) * NT]),
                    waits=[("sVE", ffn2_oc[0][oc])] if oc in (0, DC - 1) or True else (),
                    inc=("sY", 16))
            b.op("sync", lambda e: e.nop(), waits=[("sY", 16 * DC)])

        resid2 = b.v("sVE")
        space2 = [("sVE", b.v("sVE")), ("sPE", b.v("sPE")), ("sSE", b.v("sSE"))]
        ln3_ve = [None]
        b.phase("blk5", [
            lambda: ln3_ve.__setitem__(0, p_ln(t_X2, [("sVE", resid2)], t_xhat_e4b)),
            lambda: ffn2_oc.__setitem__(0, p_ffn(
                t_X2, t_xhat_e4b, ln3_ve[0], t_act2, wgbuf2, wlbuf2, False,
                space_ready=space2)),
            phase_store,
        ])

        es7.close()
        es6.close()
        esQ.close()

    return nc



def _shared_weight_layout(cfg, wg, wlin, qd, kd, vd, qu, ku, vu, ow, ob):
    c = cfg
    out = {}
    a = np.ascontiguousarray(
        wg.reshape(2, c.DC // 2, 2, 128, c.HC, 128).transpose(4, 3, 0, 1, 2, 5)
    ).reshape(c.HC, 128, 2 * c.DC * 128)
    out["wg"] = np.ascontiguousarray(a)
    a = np.ascontiguousarray(
        wlin.reshape(c.HC // 2, 2, 128, c.DC, 128).transpose(3, 2, 0, 1, 4)
    ).reshape(c.DC, 128, c.HC * 128)
    out["wlin"] = np.ascontiguousarray(a)
    qkv = []
    for w in (kd, vd, qd):
        t = w.reshape(c.RC, 128, c.DC, 128).transpose(3, 2, 0, 1)
        qkv.append(t)
    out["wqkv"] = np.ascontiguousarray(np.stack(qkv, axis=1)).reshape(128, 3 * c.DC * c.RC * 128)
    out["wqkv"] = np.ascontiguousarray(
        np.stack(qkv, axis=1).transpose(0, 1, 2, 3, 4)).reshape(128, 3 * c.DC * c.RC * 128)
    out["wqu"] = np.ascontiguousarray(qu.reshape(c.H, 128, c.R))
    out["wku"] = np.ascontiguousarray(ku.reshape(c.H, 128, c.R))
    out["wvu"] = np.ascontiguousarray(
        vu.reshape(c.H, 128, c.RC, 128).transpose(0, 3, 2, 1)).reshape(c.H, 128, c.RC * 128)
    out["wout"] = np.ascontiguousarray(
        ow.reshape(c.DC, 128, c.H, 128).transpose(0, 3, 2, 1)).reshape(c.DC, 128, c.H * 128)
    out["outb"] = np.ascontiguousarray(ob.reshape(c.DC, 128).T)
    cvec = np.zeros((128, 8), np.float32)
    cvec[:, 0] = EPS
    cvec[:, 1] = EXP_SHIFT
    cvec[:, 2] = RSQ_DH
    cvec[:, 3] = 1.0 / 1024.0
    out["cvec"] = cvec
    return out


def _mask_for_rank(cfg, r_c):
    c = cfg
    p = np.arange(128)[:, None]
    g = np.arange(c.MW)[None, :]
    mb = np.zeros((128, c.G, c.MW), BFNP)
    for r in range(c.G):
        dlt = 1 if r > r_c else 0
        mb[:, r, :] = (g >= p + 128 * (c.BND - 1) + dlt).astype(BFNP)
    return mb


def _core_x(cfg, xb, r):
    c = cfg
    xloc = xb[r::c.G][:c.NT]
    return np.ascontiguousarray(
        xloc.T.reshape(c.DC, 128, c.NT).transpose(1, 0, 2))


def run(cfg, inputs, core_ids=None, runner=run_bass_kernel_spmd):
    c = cfg
    x = np.asarray(inputs["L0_attn_x"], np.float32)
    shared = _shared_weight_layout(
        c,
        np.asarray(inputs["L0_ffn_w_gating"], np.float32),
        np.asarray(inputs["L0_ffn_w_linear"], np.float32),
        np.asarray(inputs["L0_attn_q_down_w"], np.float32),
        np.asarray(inputs["L0_attn_k_down_w"], np.float32),
        np.asarray(inputs["L0_attn_v_down_w"], np.float32),
        np.asarray(inputs["L0_attn_q_up_w"], np.float32),
        np.asarray(inputs["L0_attn_k_up_w"], np.float32),
        np.asarray(inputs["L0_attn_v_up_w"], np.float32),
        np.asarray(inputs["L0_attn_out_w"], np.float32),
        np.asarray(inputs["L0_attn_out_b"], np.float32),
    )
    masks = [_mask_for_rank(c, r) for r in range(c.G)]
    in_maps = []
    for core in range(c.NC):
        g, r = core // c.G, core % c.G
        m = dict(shared)
        m["x"] = _core_x(c, x[g], r)
        m["maskbig"] = masks[r]
        in_maps.append(m)
    nc = build_program(c)
    res = runner(nc, in_maps, core_ids=core_ids or list(range(c.NC)))
    out = np.empty((c.NG, c.G * c.NT, c.D), np.float32)
    for core in range(c.NC):
        g, r = core // c.G, core % c.G
        Y = np.asarray(res.results[core]["y"]).reshape(128, c.DC, c.NT)
        out[g, r::c.G] = Y.transpose(2, 1, 0).reshape(c.NT, c.D)
    return out


def kernel(**inputs):
    return run(FULL, inputs)
